# version 15
# baseline (speedup 1.0000x reference)
import contextlib
import numpy as np
import ml_dtypes
import concourse.bass as bass
import concourse.mybir as mybir
from concourse.bass_utils import run_bass_kernel_spmd

F32 = mybir.dt.float32
BF16 = mybir.dt.bfloat16
I32 = mybir.dt.int32
AF = mybir.ActivationFunctionType
ALU = mybir.AluOpType
AX = mybir.AxisListType
IOA = bass.IndirectOffsetOnAxis

NCORES = 8
D = 2048
T = 4096
NH = 16
HD = 128
E = 64
ELOC = 8
CAP = 1024
TOPK = 6
HS = 512
H_EXP = 256
LN_EPS = 1e-5
ATT_EPS = 1e-6
BIG = 4.0e6

NT = T // 128
NCH = T // 512
ND = D // 128


class Prog:
    ENGINES = ("sync", "tensor", "vector", "scalar", "gpsimd")

    def __init__(self, nc, ctx):
        self.nc = nc
        self.ops = {e: [] for e in self.ENGINES}
        self.sem_val = {}
        self.sems = {}
        self.ctx = ctx
        self.tok = {}

    def sem(self, name):
        if name not in self.sems:
            self.sems[name] = self.ctx.enter_context(self.nc.semaphore(name))
            self.sem_val[name] = 0
        return self.sems[name]

    def val(self, name):
        self.sem(name)
        return self.sem_val[name]

    def on(self, engine, fn=None, wait=(), inc=None, inc_amt=1, tok=None):
        wl = []
        for item in wait:
            if item is None:
                continue
            s, v = item
            self.sem(s)
            assert v <= self.sem_val[s], f"wait on future: {s} {v} > {self.sem_val[s]}"
            wl.append((s, v))
        ret = None
        if inc is not None:
            self.sem(inc)
            self.sem_val[inc] += inc_amt
            ret = (inc, self.sem_val[inc])
        self.ops[engine].append((fn, tuple(wl), inc, inc_amt))
        if tok is not None:
            assert ret is not None
            self.tok[tok] = ret
        return ret

    def dma(self, engine, fn, wait=(), tok=None):
        q = {"sync": "dS", "scalar": "dA", "gpsimd": "dG", "vector": "dV"}[engine]
        return self.on(engine, fn, wait=wait, inc=q, inc_amt=16, tok=tok)

    def t(self, name):
        return self.tok.get(name)

    def barrier(self):
        bar = "bar"
        self.sem(bar)
        qwaits = {"sync": "dS", "scalar": "dA", "gpsimd": "dG"}
        for e in self.ENGINES:
            w = ()
            if e in qwaits:
                q = qwaits[e]
                self.sem(q)
                w = ((q, self.sem_val[q]),)
            self.ops[e].append((None, w, bar, 1))
        self.sem_val[bar] += len(self.ENGINES)
        v = self.sem_val[bar]
        for e in self.ENGINES:
            self.ops[e].append((None, ((bar, v),), None, 1))

    def emit(self):
        nc = self.nc
        sems = self.sems
        with nc.Block() as block:
            for ename in self.ENGINES:
                ops = self.ops[ename]

                def mk(eng, ops=ops):
                    for fn, waits, inc, inc_amt in ops:
                        for (w, v) in waits:
                            eng.wait_ge(sems[w], v)
                        if fn is None:
                            if inc is not None:
                                eng.nop().then_inc(sems[inc], inc_amt)
                            continue
                        inst = fn(eng)
                        if inc is not None:
                            inst.then_inc(sems[inc], inc_amt)
                getattr(block, ename)(mk)


def make_host_constants(core):
    c = {}
    j = np.arange(64, dtype=np.float64)
    theta = 10000.0 ** (-j / 64.0)
    pos = np.arange(T, dtype=np.float64)
    ang = pos[None, :] * theta[:, None]
    cos = np.cos(ang)
    sin = np.sin(ang)
    c["rope_c"] = np.concatenate([cos, cos], 0).astype(ml_dtypes.bfloat16)
    c["rope_s"] = np.concatenate([-sin, sin], 0).astype(ml_dtypes.bfloat16)
    c["ident"] = np.eye(128, dtype=np.float32).astype(ml_dtypes.bfloat16)
    c["lexclT"] = np.triu(np.ones((128, 128), np.float32), 1).astype(ml_dtypes.bfloat16)
    for r in range(4):
        m = (128 * r + np.arange(128)[:, None] <= np.arange(512)[None, :])
        c[f"mask{r}"] = m.astype(ml_dtypes.bfloat16)
    c["ones_col"] = np.ones((128, 1), np.float32).astype(ml_dtypes.bfloat16)
    c["ones_row_f"] = np.ones((1, 128), np.float32)
    c["iota_p"] = np.arange(128, dtype=np.float32).reshape(128, 1)
    e = np.arange(E, dtype=np.float32)
    c["e1024"] = np.tile((e * CAP)[None, :], (128, 1)).astype(np.float32)
    c["core_base"] = np.full((128, 1), core * T, np.float32)
    c["base_gidx"] = (core * NT + np.arange(NT, dtype=np.int32)).reshape(NT, 1)
    c["slot_gidx"] = (core * 64 + np.arange(64, dtype=np.int32)).reshape(64, 1)
    ee = np.arange(E)
    gg = (ee // 8) * 4096 + (ee % 8 % 4) * 1024 + np.where(ee % 8 >= 4, 32768, 0)
    c["gcomb"] = np.tile(gg[None, :].astype(np.float32), (128, 1))
    sel = np.zeros((NT, NT * 128), np.float32)
    for t in range(NT):
        sel[t, 128 * t:128 * (t + 1)] = 1.0
    c["sel32"] = sel
    return c


CONST_SPECS = {
    "rope_c": ([128, T], BF16), "rope_s": ([128, T], BF16),
    "ident": ([128, 128], BF16), "lexclT": ([128, 128], BF16),
    "mask0": ([128, 512], BF16), "mask1": ([128, 512], BF16),
    "mask2": ([128, 512], BF16), "mask3": ([128, 512], BF16),
    "ones_col": ([128, 1], BF16), "ones_row_f": ([1, 128], F32),
    "iota_p": ([128, 1], F32), "e1024": ([128, E], F32),
    "core_base": ([128, 1], F32), "base_gidx": ([NT, 1], I32),
    "slot_gidx": ([64, 1], I32), "gcomb": ([128, E], F32),
    "sel32": ([NT, NT * 128], F32),
}


def build_program(cfg):
    nc = bass.Bass("TRN2", target_bir_lowering=False)
    ctx = contextlib.ExitStack()
    heads = cfg.get("heads", NH)
    debug = cfg.get("debug_outs", ())
    upto = cfg.get("upto", "Z")
    with_ffn = cfg.get("with_ffn", True)
    with_attn = cfg.get("with_attn", True)

    dt = nc.dram_tensor
    x_in = dt("x", [T, D], F32, kind="ExternalInput").ap()
    wqT = dt("wqT", [D, D], F32, kind="ExternalInput").ap()
    wkT = dt("wkT", [D, D], F32, kind="ExternalInput").ap()
    wvT = dt("wvT", [D, D], F32, kind="ExternalInput").ap()
    woT = dt("woT", [D, D], F32, kind="ExternalInput").ap()
    wgate = dt("wgate", [D, E], F32, kind="ExternalInput").ap()
    wsu = dt("wsu", [D, 2 * HS], F32, kind="ExternalInput").ap()
    wsd = dt("wsd", [HS, D], F32, kind="ExternalInput").ap()
    wup = dt("wup", [ELOC, D, 2 * H_EXP], F32, kind="ExternalInput").ap()
    wdn = dt("wdn", [ELOC, H_EXP, D], F32, kind="ExternalInput").ap()
    y_out = dt("y", [T, D], F32, kind="ExternalOutput").ap()
    cin = {k: dt(k, list(sh), d, kind="ExternalInput").ap() for k, (sh, d) in CONST_SPECS.items()}

    lnx_d = dt("lnx_d", [T, D], BF16).ap()
    qT_d = dt("qT_d", [NH, 128, T], BF16).ap()
    kT_d = dt("kT_d", [NH, 128, T], BF16).ap()
    v_d = dt("v_d", [NH, 128, NT, HD], BF16).ap()
    ctx_d = dt("ctx_d", [NH, T, HD], BF16).ap()
    x2_d = dt("x2_d", [T, D], F32).ap()
    shared_d = dt("shared_d", [T, D], BF16).ap()
    ln2x_d = dt("ln2x_d", [T, D], BF16).ap()
    hist_d = dt("hist_d", [NT, E], F32).ap()
    hist_all_d = dt("hist_all_d", [NCORES * NT, E], F32).ap()
    base_d = dt("base_d", [NCORES * NT, E], F32).ap()
    slot_tok_d = dt("slot_tok_d", [E * CAP // 128 + NT * TOPK + 1, 128], F32).ap()
    slot_rs_d = dt("slot_rs_d", [E * CAP // 128, 128], F32).ap()
    slot_all8_d = dt("slot_all8_d", [NCORES * E * CAP // 128, 128], F32).ap()
    slot_sum_d = dt("slot_sum_d", [E * CAP // 128, 128], F32).ap()
    slot_own_d = dt("slot_own_d", [ELOC * CAP // 128, 128], F32).ap()
    ln2x_all_d = dt("ln2x_all_d", [NCORES * T, D], BF16, addr_space="Shared").ap()
    xdisp_d = dt("xdisp_d", [ELOC * CAP, D], BF16).ap()
    odisp_d = dt("odisp_d", [ELOC * CAP, D], BF16).ap()
    odall_d = dt("odall_d", [E * CAP, D], BF16, addr_space="Shared").ap()

    DBG_SHAPES = {"x2": ([T, D], F32), "lnx": ([T, D], BF16),
                  "scores": ([128, NT * E], F32), "ln2x": ([T, D], BF16),
                  "shared": ([T, D], BF16), "slot_tok": ([E * CAP, 1], F32),
                  "odisp": ([ELOC * CAP, D], BF16), "wfin": ([128, NT * TOPK], F32),
                  "gidx": ([128, NT * TOPK], I32), "hist_all": ([NCORES * NT, E], F32),
                  "base": ([NCORES * NT, E], F32), "qt": ([128, T], BF16),
                  "kt": ([128, T], BF16), "vt": ([128, NT * HD], BF16),
                  "ctxt": ([128, 512], BF16)}
    dbg = {}
    for name in debug:
        sh, d = DBG_SHAPES[name]
        dbg[name] = dt("dbg_" + name, sh, d, kind="ExternalOutput").ap()

    P = Prog(nc, ctx)
    rg = [list(range(NCORES))]

    def VC(fn, wait=(), tok=None):
        w = [x for x in wait if x is not None] + [("va", P.val("va"))]
        return P.on("vector", fn, wait=w, inc="va", tok=tok)

    cs = {}
    for k, (sh, d) in CONST_SPECS.items():
        if k in ("rope_c", "rope_s", "sel32"):
            continue
        cs[k] = ctx.enter_context(nc.sbuf_tensor("c_" + k, list(sh), d))
        P.dma("sync", lambda e, a=cs[k], b=cin[k]: e.dma_start(out=a[:], in_=b[:, :]))
    tok_const = ("dS", P.val("dS"))

    if with_attn:
      with contextlib.ExitStack() as actx:
        asb = lambda n, sh, d: actx.enter_context(nc.sbuf_tensor(n, sh, d))
        rope_c = asb("rope_c_sb", [128, T], BF16)
        rope_s = asb("rope_s_sb", [128, T], BF16)
        P.dma("sync", lambda e: e.dma_start(out=rope_c[:], in_=cin["rope_c"][:, :]))
        P.dma("sync", lambda e: e.dma_start(out=rope_s[:], in_=cin["rope_s"][:, :]), tok="ropeld")

        xt = [asb(f"xt{i}", [128, D], F32) for i in range(2)]
        lnxt = [asb(f"lnxt{i}", [128, D], BF16) for i in range(2)]
        stat = asb("stat", [128, 8], F32)
        sq_scr = asb("sq_scr", [128, D], F32)
        lnxT = asb("lnxT", [128, ND, 2048], BF16)
        wsb = [asb(f"wsb{i}", [128, ND, 1024], BF16) for i in range(2)]
        qs_s = asb("qs_s", [128, 512], BF16)
        qev = asb("qev", [128, 512], BF16)
        t1b = asb("t1b", [128, 512], BF16)
        rot = asb("rot", [128, 512], BF16)
        pj = [actx.enter_context(nc.psum_tensor(f"pj{i}", [128, 512], F32)) for i in range(2)]

        for tt in range(NT):
            par = tt % 2
            b = xt[par]
            ld = P.dma("sync", lambda e, b=b, tt=tt: e.dma_start(
                out=b[:], in_=x_in[128 * tt:128 * (tt + 1), :]),
                wait=[P.t(f"xt{par}_free")], tok=f"xt{par}_rdy")
            P.on("scalar", lambda e, b=b: e.activation(
                out=sq_scr[:], in_=b[:], func=AF.Square, accum_out=stat[:, 1:2]),
                wait=[ld, P.t("stat1_free")], inc="sa")
            sa_sq = ("sa", P.val("sa"))
            VC(lambda e, b=b: e.reduce_sum(stat[:, 0:1], b[:], axis=AX.X), wait=[ld])
            VC(lambda e: e.tensor_scalar_mul(stat[:, 2:3], stat[:, 0:1], 1.0 / D))
            VC(lambda e: e.tensor_scalar_mul(stat[:, 3:4], stat[:, 1:2], 1.0 / D),
               wait=[sa_sq], tok="stat1_free")
            VC(lambda e: e.tensor_mul(stat[:, 4:5], stat[:, 2:3], stat[:, 2:3]))
            VC(lambda e: e.tensor_sub(stat[:, 3:4], stat[:, 3:4], stat[:, 4:5]))
            VC(lambda e: e.tensor_scalar_add(stat[:, 3:4], stat[:, 3:4], LN_EPS), tok="var_rdy")
            P.on("scalar", lambda e: e.activation(out=stat[:, 5:6], in_=stat[:, 3:4], func=AF.Sqrt),
                 wait=[P.t("var_rdy")], inc="sa", tok="sqrt_rdy")
            VC(lambda e: e.reciprocal(stat[:, 6:7], stat[:, 5:6]), wait=[P.t("sqrt_rdy")])
            lno = VC(lambda e, b=b, o=lnxt[par]: e.tensor_scalar(
                o[:], b[:], stat[:, 2:3], stat[:, 6:7], ALU.subtract, ALU.mult),
                wait=[P.t(f"lnxt{par}_free")], tok=f"xt{par}_free")
            P.dma("sync", lambda e, o=lnxt[par], tt=tt: e.dma_start(
                out=lnx_d[128 * tt:128 * (tt + 1), :], in_=o[:]),
                wait=[lno], tok=f"lnxt{par}_free")
            if tt == NT // 2 - 1:
                P.tok["lnx_half0"] = ("dS", P.val("dS"))
        lnx_all = ("dS", P.val("dS"))

        def load_whalf(par, wT, oh):
            return P.dma("gpsimd", lambda e, wT=wT, oh=oh, par=par: e.dma_start(
                out=wsb[par][:],
                in_=wT[:, 1024 * oh:1024 * (oh + 1)].rearrange("(dt p) o -> p dt o", p=128)),
                wait=[P.t(f"wsb{par}_free")], tok=f"wsb{par}_rdy")

        jobs = []
        for wi, (wT, outd) in enumerate([(wqT, qT_d), (wkT, kT_d)]):
            jobs.append(("qk", wT, outd, 0))
            jobs.append(("qk", wT, outd, 1))
        jobs.append(("v", wvT, None, 0))
        jobs.append(("v", wvT, None, 1))

        for half in range(2):
            t0 = 2048 * half
            for ct in range(ND):
                P.dma("scalar", lambda e, ct=ct, t0=t0: e.dma_start(
                    out=lnxT[:, ct, :], in_=lnx_d[t0:t0 + 2048, 128 * ct:128 * (ct + 1)],
                    transpose=True),
                    wait=[P.t("lnx_half0") if half == 0 else lnx_all,
                          P.t("lnxT_free")] if ct == 0 else [])
            ltdone = ("dA", P.val("dA"))

            for ji, (kind, wT, outd, oh) in enumerate(jobs):
                par = ji % 2
                load_whalf(par, wT, oh)
                wrdy = P.t(f"wsb{par}_rdy")
                if kind == "qk":
                    for oo in range(8):
                        o = 8 * oh + oo
                        for c4 in range(4):
                            pp = c4 % 2
                            for dtl in range(ND):
                                P.on("tensor", lambda e, par=par, pp=pp, dtl=dtl, c4=c4, oo=oo:
                                     e.matmul(
                                         pj[pp][:], wsb[par][:, dtl, 128 * oo:128 * (oo + 1)],
                                         lnxT[:, dtl, 512 * c4:512 * (c4 + 1)],
                                         start=(dtl == 0), stop=(dtl == ND - 1)),
                                     wait=[wrdy, ltdone, P.t(f"pj{pp}_free")] if dtl == 0 else [],
                                     inc="tm" if dtl == ND - 1 else None)
                            mmv = ("tm", P.val("tm"))
                            tg = t0 + 512 * c4
                            P.on("scalar", lambda e, pp=pp: e.copy(out=qev[:], in_=pj[pp][:]),
                                 wait=[mmv, P.t("qev_free")], inc="sa", tok=f"pj{pp}_free")
                            ev = ("sa", P.val("sa"))
                            VC(lambda e: e.tensor_copy(qs_s[0:64, :], qev[64:128, :]),
                               wait=[ev, P.t("ropeld"), P.t("rot_free"), P.t("rope_pool")])
                            qsv = VC(lambda e: e.tensor_copy(qs_s[64:128, :], qev[0:64, :]))
                            pv = P.on("gpsimd", lambda e, tg=tg: e.tensor_mul(
                                rot[:], qs_s[:], rope_s[:, tg:tg + 512]),
                                wait=[qsv, P.t("rot_free")], inc="pa", tok="rope_pool")
                            VC(lambda e, tg=tg: e.tensor_mul(
                                t1b[:], qev[:], rope_c[:, tg:tg + 512]), tok="qev_free")
                            ro = VC(lambda e: e.tensor_add(rot[:], rot[:], t1b[:]), wait=[pv])
                            P.dma("sync", lambda e, outd=outd, o=o, tg=tg: e.dma_start(
                                out=outd[o, :, tg:tg + 512], in_=rot[:]), wait=[ro],
                                tok="rot_free")
                else:
                    for oc in range(2):
                        for tt in range(16):
                            pp = tt % 2
                            for dtl in range(ND):
                                P.on("tensor", lambda e, par=par, pp=pp, dtl=dtl, tt=tt, oc=oc:
                                     e.matmul(
                                         pj[pp][:], lnxT[:, dtl, 128 * tt:128 * (tt + 1)],
                                         wsb[par][:, dtl, 512 * oc:512 * (oc + 1)],
                                         start=(dtl == 0), stop=(dtl == ND - 1)),
                                     wait=[wrdy, P.t(f"pj{pp}_free")] if dtl == 0 else [],
                                     inc="tm" if dtl == ND - 1 else None)
                            mmv = ("tm", P.val("tm"))
                            ev = P.on("scalar", lambda e, pp=pp: e.copy(out=qev[:], in_=pj[pp][:]),
                                      wait=[mmv, P.t("qev_free")], inc="sa", tok=f"pj{pp}_free")
                            tg = t0 + 128 * tt
                            for hh in range(4):
                                h = 8 * oh + 4 * oc + hh
                                last = P.dma("sync", lambda e, h=h, ti=tg // 128, hh=hh:
                                             e.dma_start(
                                                 out=v_d[h, :, ti, :],
                                                 in_=qev[:, 128 * hh:128 * (hh + 1)]),
                                             wait=[ev] if hh == 0 else [])
                            P.tok["qev_free"] = last
                P.tok[f"wsb{par}_free"] = ("tm", P.val("tm"))
            P.tok["lnxT_free"] = ("tm", P.val("tm"))

      P.barrier()

      with contextlib.ExitStack() as bctx:
       if upto not in ("A",):
        bsb = lambda n, sh, d: bctx.enter_context(nc.sbuf_tensor(n, sh, d))
        qT = [bsb(f"qT{i}", [128, T], BF16) for i in range(2)]
        kT = [bsb(f"kT{i}", [128, T], BF16) for i in range(2)]
        vaug = [bsb(f"vaug{i}", [128, NT, 132], BF16) for i in range(2)]
        s2b = [bsb(f"s2b{i}", [128, 512], BF16) for i in range(6)]
        s4b = [bsb(f"s4b{i}", [128, 512], BF16) for i in range(6)]
        dstat = bsb("dstat", [128, 4], F32)
        ctxst = bsb("ctxst", [128, 4, 128], BF16)
        psS = [bctx.enter_context(nc.psum_tensor(f"psS{i}", [128, 512], F32)) for i in range(4)]
        psC = [bctx.enter_context(nc.psum_tensor(f"psC{i}", [128, 512], F32)) for i in range(4)]

        mask_cs = [cs[f"mask{r}"] for r in range(4)]

        for h in range(heads):
            par = h % 2
            qb, kb, vb = qT[par], kT[par], vaug[par]
            P.dma("sync", lambda e, h=h, qb=qb: e.dma_start(out=qb[:], in_=qT_d[h, :, :]),
                  wait=[P.t(f"qkv{par}_free")])
            P.dma("sync", lambda e, h=h, kb=kb: e.dma_start(out=kb[:], in_=kT_d[h, :, :]))
            mz = P.on("vector", lambda e, vb=vb: e.memset(vb[:], 1.0),
                      wait=[P.t(f"qkv{par}_free")], inc="va")
            hl = P.dma("sync", lambda e, h=h, vb=vb: e.dma_start(
                out=vb[:, :, 0:128], in_=v_d[h, :, :, :]),
                wait=[mz], tok="hload")
            hload = ("dS", P.val("dS"))

            for c in range(NCH):
                ntk = 4 * (c + 1)
                LAG = 6
                s4vs = {}

                def emit_ctx(i, ntk=ntk, vb=vb):
                    sp = i % 6
                    for jj in range(4):
                        P.on("tensor", lambda e, sp=sp, jj=jj, i=i, ntk=ntk, vb=vb: e.matmul(
                            psC[jj][:, 0:129], s4b[sp][:, 128 * jj:128 * (jj + 1)],
                            vb[:, i, 0:129], start=(i == 0), stop=(i == ntk - 1)),
                            wait=([s4vs[i]] + ([P.t("ctx_free")] if i == 0 and jj == 0 else []))
                            if jj == 0 else [],
                            inc="tm" if jj == 3 else None,
                            tok=f"s4b{sp}_free" if jj == 3 else None)

                for i in range(ntk):
                    sp = i % 6
                    spS = i % 4
                    P.on("tensor", lambda e, kb=kb, qb=qb, spS=spS, i=i, c=c: e.matmul(
                        psS[spS][:], kb[:, 128 * i:128 * (i + 1)], qb[:, 512 * c:512 * (c + 1)],
                        start=True, stop=True),
                        wait=[hload, P.t(f"psS{spS}_free")], inc="tm")
                    smm = ("tm", P.val("tm"))
                    if i >= LAG:
                        emit_ctx(i - LAG)
                    P.on("scalar", lambda e, sp=sp, spS=spS: e.activation(
                        out=s2b[sp][:], in_=psS[spS][:], func=AF.Square),
                        wait=[smm, P.t(f"s2b{sp}_free")], inc="sa", tok=f"psS{spS}_free")
                    sqv = ("sa", P.val("sa"))
                    wts = [sqv, P.t(f"s4b{sp}_free")]
                    if i >= 4 * c:
                        r = i - 4 * c
                        VC(lambda e, sp=sp, r=r: e.tensor_mul(
                            s2b[sp][:], s2b[sp][:], mask_cs[r][:]), wait=wts)
                        wts = []
                        s4vs[i] = VC(lambda e, sp=sp: e.tensor_mul(
                            s4b[sp][:], s2b[sp][:], s2b[sp][:]),
                            wait=wts, tok=f"s2b{sp}_free")
                    elif i % 2 == 1:
                        s4vs[i] = P.on("gpsimd", lambda e, sp=sp: e.tensor_mul(
                            s4b[sp][:], s2b[sp][:], s2b[sp][:]),
                            wait=wts, inc="pa", tok=f"s2b{sp}_free")
                    else:
                        s4vs[i] = VC(lambda e, sp=sp: e.tensor_mul(
                            s4b[sp][:], s2b[sp][:], s2b[sp][:]),
                            wait=wts, tok=f"s2b{sp}_free")
                for i in range(max(0, ntk - LAG), ntk):
                    emit_ctx(i)
                ctxdone = ("tm", P.val("tm"))
                for jj in range(4):
                    VC(lambda e, jj=jj: e.tensor_scalar_max(
                        dstat[:, jj:jj + 1], psC[jj][:, 128:129], ATT_EPS),
                        wait=[ctxdone, P.t("ctxst_free")] if jj == 0 else [])
                rcv = VC(lambda e: e.reciprocal(dstat[:, 0:4], dstat[:, 0:4]))
                dv = None
                for jj in range(4):
                    dv = P.on("scalar", lambda e, jj=jj: e.activation(
                        out=ctxst[:, jj, :], in_=psC[jj][:, 0:128], func=AF.Copy,
                        scale=dstat[:, jj:jj + 1]),
                        wait=[rcv] if jj == 0 else [], inc="sa")
                P.tok["ctx_free"] = dv
                stv = ("sa", P.val("sa"))
                P.dma("sync", lambda e, c=c, h=h: e.dma_start(
                    out=ctx_d[h, 512 * c:512 * (c + 1), :].rearrange(
                        "(jj p) hd -> p jj hd", p=128),
                    in_=ctxst[:]), wait=[stv], tok="ctxst_free")
            P.tok[f"qkv{par}_free"] = ("tm", P.val("tm"))

      P.barrier()

      with contextlib.ExitStack() as cctx:
       if upto not in ("A", "B"):
        csb = lambda n, sh, d: cctx.enter_context(nc.sbuf_tensor(n, sh, d))
        ctxc = [csb(f"ctxc{i}", [128, NH, 512], BF16) for i in range(2)]
        woTsC = csb("woTsC", [128, ND, D], BF16)
        xrl = [csb(f"xrl{i}", [128, D], F32) for i in range(2)]
        ao = csb("ao", [128, D], F32)
        x2t = csb("x2t", [128, D], F32)
        ln2t = csb("ln2t", [128, D], BF16)
        stat2 = csb("stat2", [128, 12], F32)
        sqs = csb("sqs", [128, 512], F32)
        psO = [cctx.enter_context(nc.psum_tensor(f"psO{i}", [128, 512], F32)) for i in range(8)]

        wo2 = P.dma("gpsimd", lambda e: e.dma_start(
            out=woTsC[:], in_=woT[:, :].rearrange("(dt p) o -> p dt o", p=128)))

        for c in range(NCH):
            par = c % 2
            cb = ctxc[par]
            for h in range(NH):
                P.dma("scalar", lambda e, c=c, cb=cb, h=h: e.dma_start(
                    out=cb[:, h, :], in_=ctx_d[h, 512 * c:512 * (c + 1), :],
                    transpose=True),
                    wait=[P.t(f"ctxc{par}_free")] if h == 0 else [])
            cc = ("dA", P.val("dA"))
            for tt4 in range(4):
                tt = 4 * c + tt4
                xpar = tt % 2
                xb = xrl[xpar]
                xr = P.dma("sync", lambda e, tt=tt, xb=xb: e.dma_start(
                    out=xb[:], in_=x_in[128 * tt:128 * (tt + 1), :]),
                    wait=[P.t(f"xrl{xpar}_free")])
                ob = 4 * (tt % 2)
                for oc in range(4):
                    for dtl in range(ND):
                        P.on("tensor", lambda e, cb=cb, oc=oc, ob=ob, dtl=dtl, tt4=tt4: e.matmul(
                            psO[ob + oc][:], cb[:, dtl, 128 * tt4:128 * (tt4 + 1)],
                            woTsC[:, dtl, 512 * oc:512 * (oc + 1)],
                            start=(dtl == 0), stop=(dtl == ND - 1)),
                            wait=[cc, wo2, P.t(f"psO{ob + oc}_free")] if dtl == 0 else [],
                            inc="tm" if dtl == ND - 1 else None)
                if tt4 == 3:
                    P.tok[f"ctxc{par}_free"] = ("tm", P.val("tm"))
                omm = ("tm", P.val("tm"))
                sq_done = []
                for oc in range(4):
                    P.on("scalar", lambda e, oc=oc, ob=ob: e.activation(
                        out=sqs[:], in_=psO[ob + oc][:], func=AF.Square,
                        accum_out=stat2[:, 4 + oc:5 + oc]),
                        wait=[omm, P.t("stat2b_free")] if oc == 0 else [],
                        inc="sa")
                    sq_done.append(("sa", P.val("sa")))
                for oc in range(4):
                    VC(lambda e, oc=oc, ob=ob: e.reduce_sum(
                        stat2[:, oc:oc + 1], psO[ob + oc][:], axis=AX.X),
                        wait=[sq_done[oc], P.t("stat2a_free")] if oc == 0 else [sq_done[oc]])
                sqv = ("sa", P.val("sa"))
                VC(lambda e: e.reduce_sum(stat2[:, 8:9], stat2[:, 0:4], axis=AX.X),
                   tok="stat2a_free")
                VC(lambda e: e.reduce_sum(stat2[:, 9:10], stat2[:, 4:8], axis=AX.X),
                   wait=[sqv], tok="stat2b_free")
                VC(lambda e: e.tensor_scalar_mul(stat2[:, 8:9], stat2[:, 8:9], 1.0 / D))
                VC(lambda e: e.tensor_scalar_mul(stat2[:, 9:10], stat2[:, 9:10], 1.0 / D))
                VC(lambda e: e.tensor_mul(stat2[:, 10:11], stat2[:, 8:9], stat2[:, 8:9]))
                VC(lambda e: e.tensor_sub(stat2[:, 9:10], stat2[:, 9:10], stat2[:, 10:11]))
                VC(lambda e: e.tensor_scalar_add(stat2[:, 9:10], stat2[:, 9:10], LN_EPS),
                   tok="varO")
                P.on("scalar", lambda e: e.activation(
                    out=stat2[:, 10:11], in_=stat2[:, 9:10], func=AF.Sqrt),
                    wait=[P.t("varO")], inc="sa", tok="sqrtO")
                VC(lambda e: e.reciprocal(stat2[:, 11:12], stat2[:, 10:11]),
                   wait=[P.t("sqrtO")])
                lastao = None
                for oc in range(4):
                    lastao = VC(lambda e, oc=oc, ob=ob: e.tensor_scalar(
                        ao[:, 512 * oc:512 * (oc + 1)], psO[ob + oc][:], stat2[:, 8:9],
                        stat2[:, 11:12], ALU.subtract, ALU.mult))
                for oc in range(4):
                    P.tok[f"psO{ob + oc}_free"] = lastao
                x2v = VC(lambda e, xb=xb: e.tensor_add(x2t[:], ao[:], xb[:]),
                         wait=[xr, P.t("x2dma"), P.t("x2sq")], tok=f"xrl{xpar}_free")
                P.dma("sync", lambda e, tt=tt: e.dma_start(
                    out=x2_d[128 * tt:128 * (tt + 1), :], in_=x2t[:]), wait=[x2v], tok="x2dma")
                VC(lambda e: e.reduce_sum(stat2[:, 0:1], x2t[:], axis=AX.X))
                for oc in range(4):
                    P.on("scalar", lambda e, oc=oc: e.activation(
                        out=sqs[:], in_=x2t[:, 512 * oc:512 * (oc + 1)], func=AF.Square,
                        accum_out=stat2[:, 4 + oc:5 + oc]),
                        wait=[x2v] if oc == 0 else [], inc="sa" if oc == 3 else None,
                        tok="x2sq" if oc == 3 else None)
                sq2 = ("sa", P.val("sa"))
                VC(lambda e: e.reduce_sum(stat2[:, 5:6], stat2[:, 4:8], axis=AX.X), wait=[sq2])
                VC(lambda e: e.tensor_scalar_mul(stat2[:, 0:1], stat2[:, 0:1], 1.0 / D))
                VC(lambda e: e.tensor_scalar_mul(stat2[:, 5:6], stat2[:, 5:6], 1.0 / D))
                VC(lambda e: e.tensor_mul(stat2[:, 6:7], stat2[:, 0:1], stat2[:, 0:1]))
                VC(lambda e: e.tensor_sub(stat2[:, 5:6], stat2[:, 5:6], stat2[:, 6:7]))
                VC(lambda e: e.tensor_scalar_add(stat2[:, 5:6], stat2[:, 5:6], LN_EPS), tok="var2")
                P.on("scalar", lambda e: e.activation(
                    out=stat2[:, 6:7], in_=stat2[:, 5:6], func=AF.Sqrt),
                    wait=[P.t("var2")], inc="sa", tok="sqrt2")
                VC(lambda e: e.reciprocal(stat2[:, 7:8], stat2[:, 6:7]), wait=[P.t("sqrt2")])
                l2 = VC(lambda e: e.tensor_scalar(
                    ln2t[:], x2t[:], stat2[:, 0:1], stat2[:, 7:8], ALU.subtract, ALU.mult),
                    wait=[P.t("ln2dma")])
                P.dma("sync", lambda e, tt=tt: e.dma_start(
                    out=ln2x_d[128 * tt:128 * (tt + 1), :], in_=ln2t[:]), wait=[l2], tok="ln2dma")

      P.barrier()
    else:
        with contextlib.ExitStack() as ectx:
            esb = lambda n, sh, d: ectx.enter_context(nc.sbuf_tensor(n, sh, d))
            xt0 = esb("xt0", [128, D], F32)
            ln2t0 = esb("ln2t0", [128, D], BF16)
            st0 = esb("st0", [128, 8], F32)
            sqs0 = esb("sqs0", [128, D], F32)
            for tt in range(NT):
                ld = P.dma("sync", lambda e, tt=tt: e.dma_start(
                    out=xt0[:], in_=x_in[128 * tt:128 * (tt + 1), :]),
                    wait=[P.t("xt0_free"), P.t("x2c0")])
                P.dma("sync", lambda e, tt=tt: e.dma_start(
                    out=x2_d[128 * tt:128 * (tt + 1), :], in_=xt0[:]), wait=[ld], tok="x2c0")
                VC(lambda e: e.reduce_sum(st0[:, 0:1], xt0[:], axis=AX.X), wait=[ld])
                P.on("scalar", lambda e: e.activation(
                    out=sqs0[:], in_=xt0[:], func=AF.Square, accum_out=st0[:, 1:2]),
                    wait=[ld], inc="sa")
                sq = ("sa", P.val("sa"))
                VC(lambda e: e.tensor_scalar_mul(st0[:, 2:3], st0[:, 0:1], 1.0 / D))
                VC(lambda e: e.tensor_scalar_mul(st0[:, 3:4], st0[:, 1:2], 1.0 / D), wait=[sq])
                VC(lambda e: e.tensor_mul(st0[:, 4:5], st0[:, 2:3], st0[:, 2:3]))
                VC(lambda e: e.tensor_sub(st0[:, 3:4], st0[:, 3:4], st0[:, 4:5]))
                VC(lambda e: e.tensor_scalar_add(st0[:, 3:4], st0[:, 3:4], LN_EPS), tok="v0")
                P.on("scalar", lambda e: e.activation(out=st0[:, 5:6], in_=st0[:, 3:4], func=AF.Sqrt),
                     wait=[P.t("v0")], inc="sa", tok="s0")
                VC(lambda e: e.reciprocal(st0[:, 6:7], st0[:, 5:6]), wait=[P.t("s0")])
                l2 = VC(lambda e: e.tensor_scalar(
                    ln2t0[:], xt0[:], st0[:, 2:3], st0[:, 6:7], ALU.subtract, ALU.mult),
                    wait=[P.t("ln2dma0")], tok="xt0_free")
                P.dma("sync", lambda e, tt=tt: e.dma_start(
                    out=ln2x_d[128 * tt:128 * (tt + 1), :], in_=ln2t0[:]), wait=[l2], tok="ln2dma0")
        P.barrier()

    if "qt" in dbg:
        P.dma("gpsimd", lambda e: e.dma_start(out=dbg["qt"][:, :], in_=qT_d[0, :, :]))
        P.dma("gpsimd", lambda e: e.dma_start(out=dbg["kt"][:, :], in_=kT_d[0, :, :]))
        P.dma("gpsimd", lambda e: e.dma_start(out=dbg["vt"][:, :], in_=v_d[0, :, :, :]))
    if "ctxt" in dbg:
        P.dma("gpsimd", lambda e: e.dma_start(
            out=dbg["ctxt"][:, :].rearrange("p (jj hd) -> (jj p) hd", hd=HD),
            in_=ctx_d[0, 3584:4096, :]))
    if upto in ("A", "B"):
        if "lnx" in dbg:
            P.dma("gpsimd", lambda e: e.dma_start(out=dbg["lnx"][:, :], in_=lnx_d[:, :]))
        fin = [("dS", P.val("dS")), ("dG", P.val("dG")), ("dA", P.val("dA"))]
        P.on("sync", fn=None, wait=fin)
        P.emit()
        return nc, ctx
    if "x2" in dbg:
        P.dma("gpsimd", lambda e: e.dma_start(out=dbg["x2"][:, :], in_=x2_d[:, :]))
    if "lnx" in dbg:
        P.dma("gpsimd", lambda e: e.dma_start(out=dbg["lnx"][:, :], in_=lnx_d[:, :]))
    if "ln2x" in dbg:
        P.dma("gpsimd", lambda e: e.dma_start(out=dbg["ln2x"][:, :], in_=ln2x_d[:, :]))

    if not with_ffn:
        fin = [("dS", P.val("dS")), ("dG", P.val("dG"))]
        P.on("sync", fn=None, wait=fin)
        P.emit()
        return nc, ctx

    agl = P.on("gpsimd", lambda e: e.collective_compute(
        "AllGather", ALU.bypass, rg, [ln2x_d[:, :].opt()], [ln2x_all_d[:, :].opt()]),
        wait=[("dS", P.val("dS"))], inc="cc")

    with contextlib.ExitStack() as dctx:
        dsb = lambda n, sh, d: dctx.enter_context(nc.sbuf_tensor(n, sh, d))
        scores_all = dsb("scores_all", [128, NT * E], F32)
        mx8_all = dsb("mx8_all", [128, NT * 8], F32)
        wvec_all = dsb("wvec_all", [128, NT * TOPK], F32)
        P_all = dsb("P_all", [128, NT * E], BF16)
        wfin_all = dsb("wfin_all", [128, NT * TOPK], F32)
        gidx_all = dsb("gidx_all", [128, NT * TOPK], I32)
        hist_sb = dsb("hist_sb", [1, NT * E], F32)
        mstat = dsb("mstat", [128, 4], F32)

        with contextlib.ExitStack() as d1c:
            d1sb = lambda n, sh, d: d1c.enter_context(nc.sbuf_tensor(n, sh, d))
            ln2xT = [d1sb(f"ln2xT{i}", [128, ND, 512], BF16) for i in range(2)]
            wgs = d1sb("wgs", [128, ND, E], BF16)
            wsus = d1sb("wsus", [128, ND, 2 * HS], BF16)
            wsds = d1sb("wsds", [128, 4, D], BF16)
            hT = d1sb("hT", [128, 4, 512], BF16)
            gsil = d1sb("gsil", [128, 512], BF16)
            oh6 = d1sb("oh6", [128, E], BF16)
            shst = d1sb("shst", [128, D], BF16)
            zrow = d1sb("zrow", [128, 512], F32)
            psG = d1c.enter_context(nc.psum_tensor("psG", [128, E], F32))
            psH = d1c.enter_context(nc.psum_tensor("psH", [1, E], F32))
            psP = d1c.enter_context(nc.psum_tensor("psP", [128, E], F32))
            psU = [d1c.enter_context(nc.psum_tensor(f"psU{i}", [128, 512], F32)) for i in range(2)]
            psDn = d1c.enter_context(nc.psum_tensor("psDn", [128, 512], F32))

            wg1 = P.dma("gpsimd", lambda e: e.dma_start(
                out=wgs[:], in_=wgate[:, :].rearrange("(dt p) o -> p dt o", p=128)))
            wg2 = P.dma("gpsimd", lambda e: e.dma_start(
                out=wsus[:], in_=wsu[:, :].rearrange("(dt p) o -> p dt o", p=128)))
            wg3 = P.dma("gpsimd", lambda e: e.dma_start(
                out=wsds[:], in_=wsd[:, :].rearrange("(ht p) o -> p ht o", p=128)))
            zz = P.on("vector", lambda e: e.memset(zrow[:], 0.0), inc="va")
            P.dma("sync", lambda e: e.dma_start(
                out=slot_tok_d[0:E * CAP // 128, :].rearrange("(p a) o -> p (a o)", p=128),
                in_=zrow[:]), wait=[zz], tok="slotz")

            for c in range(NCH):
                par = c % 2
                lb = ln2xT[par]
                for ct in range(ND):
                    P.dma("scalar", lambda e, ct=ct, c=c, lb=lb: e.dma_start(
                        out=lb[:, ct, :], in_=ln2x_d[512 * c:512 * (c + 1), 128 * ct:128 * (ct + 1)],
                        transpose=True),
                        wait=[P.t(f"ln2xT{par}_free")] if ct == 0 else [])
                ltv = ("dA", P.val("dA"))
                for ht in range(8):
                    pp = ht % 2
                    for dtl in range(ND):
                        P.on("tensor", lambda e, lb=lb, pp=pp, dtl=dtl, ht=ht: e.matmul(
                            psU[pp][:], wsus[:, dtl, 128 * ht:128 * (ht + 1)], lb[:, dtl, :],
                            start=(dtl == 0), stop=(dtl == ND - 1)),
                            wait=[ltv, wg2, P.t(f"psU{pp}_free")] if dtl == 0 else [],
                            inc="tm" if dtl == ND - 1 else None)
                    suv = ("tm", P.val("tm"))
                    if ht < 4:
                        P.on("scalar", lambda e, pp=pp, ht=ht: e.copy(out=hT[:, ht, :], in_=psU[pp][:]),
                             wait=[suv, P.t(f"hTy{ht}_free")], inc="sa", tok=f"psU{pp}_free")
                    else:
                        P.on("scalar", lambda e, pp=pp: e.activation(
                            out=gsil[:], in_=psU[pp][:], func=AF.Silu),
                            wait=[suv, P.t("gsil_free")], inc="sa", tok=f"psU{pp}_free")
                        sgv = ("sa", P.val("sa"))
                        P.on("vector", lambda e, ht=ht: e.tensor_mul(
                            hT[:, ht - 4, :], hT[:, ht - 4, :], gsil[:]),
                            wait=[sgv], inc="va", tok="gsil_free")
                hTv = ("va", P.val("va"))
                for tt4 in range(4):
                    for oc in range(4):
                        for hs4 in range(4):
                            P.on("tensor", lambda e, hs4=hs4, tt4=tt4, oc=oc: e.matmul(
                                psDn[:], hT[:, hs4, 128 * tt4:128 * (tt4 + 1)],
                                wsds[:, hs4, 512 * oc:512 * (oc + 1)],
                                start=(hs4 == 0), stop=(hs4 == 3)),
                                wait=[hTv, wg3, P.t("psDn_free")] if hs4 == 0 else [],
                                inc="tm" if hs4 == 3 else None)
                        P.on("scalar", lambda e, oc=oc: e.copy(
                            out=shst[:, 512 * oc:512 * (oc + 1)], in_=psDn[:]),
                            wait=[("tm", P.val("tm")), P.t("shst_free")] if oc == 0
                            else [("tm", P.val("tm"))],
                            inc="sa", tok="psDn_free")
                    shv = ("sa", P.val("sa"))
                    P.dma("sync", lambda e, c=c, tt4=tt4: e.dma_start(
                        out=shared_d[128 * (4 * c + tt4):128 * (4 * c + tt4 + 1), :], in_=shst[:]),
                        wait=[shv], tok="shst_free")
                if c == NCH - 1:
                    for ht in range(4):
                        P.tok[f"hTy{ht}_free"] = ("tm", P.val("tm"))
                else:
                    for ht in range(4):
                        P.tok[f"hTy{ht}_free"] = ("tm", P.val("tm"))
                for tt4 in range(4):
                    tt = 4 * c + tt4
                    for dtl in range(ND):
                        P.on("tensor", lambda e, lb=lb, dtl=dtl, tt4=tt4: e.matmul(
                            psG[:], lb[:, dtl, 128 * tt4:128 * (tt4 + 1)], wgs[:, dtl, :],
                            start=(dtl == 0), stop=(dtl == ND - 1)),
                            wait=[ltv, wg1, P.t("psG_free")] if dtl == 0 else [],
                            inc="tm" if dtl == ND - 1 else None)
                    gmm = ("tm", P.val("tm"))
                    P.on("scalar", lambda e, tt=tt: e.activation(
                        out=scores_all[:, E * tt:E * (tt + 1)], in_=psG[:], func=AF.Sigmoid),
                        wait=[gmm], inc="sa", tok="psG_free")
                    sgv = ("sa", P.val("sa"))
                    VC(lambda e, tt=tt: e.max(
                        out=mx8_all[:, 8 * tt:8 * (tt + 1)], in_=scores_all[:, E * tt:E * (tt + 1)]),
                        wait=[sgv])
                    VC(lambda e, tt=tt: e.reduce_sum(
                        mstat[:, 0:1], mx8_all[:, 8 * tt:8 * tt + 6], axis=AX.X))
                    VC(lambda e: e.reciprocal(mstat[:, 1:2], mstat[:, 0:1]))
                    VC(lambda e, tt=tt: e.tensor_scalar_mul(
                        wvec_all[:, TOPK * tt:TOPK * (tt + 1)], mx8_all[:, 8 * tt:8 * tt + 6],
                        mstat[:, 1:2]))
                    ohv = VC(lambda e, tt=tt: e.tensor_single_scalar(
                        oh6[:], scores_all[:, E * tt:E * (tt + 1)],
                        mx8_all[:, 8 * tt + 5:8 * tt + 6], ALU.is_ge),
                        wait=[P.t("oh6_free")])
                    P.on("tensor", lambda e: e.matmul(
                        psH[:], cs["ones_col"][:], oh6[:], start=True, stop=True),
                        wait=[ohv, tok_const, P.t("psH_free")], inc="tm")
                    hmm = ("tm", P.val("tm"))
                    P.on("tensor", lambda e: e.matmul(
                        psP[:], cs["lexclT"][:], oh6[:], start=True, stop=True),
                        wait=[P.t("psP_free")], inc="tm", tok="oh6_free")
                    pmm = ("tm", P.val("tm"))
                    P.on("vector", lambda e, tt=tt: e.tensor_copy(
                        hist_sb[0:1, E * tt:E * (tt + 1)], psH[:]),
                        wait=[hmm], inc="va", tok="psH_free")
                    P.on("vector", lambda e, tt=tt: e.tensor_copy(
                        P_all[:, E * tt:E * (tt + 1)], psP[:]),
                        wait=[pmm], inc="va", tok="psP_free")
            hsv = ("va", P.val("va"))
            P.dma("sync", lambda e: e.dma_start(
                out=hist_d[:, :].rearrange("a b -> (a b)"), in_=hist_sb[0:1, :]),
                wait=[hsv], tok="hist_out")

        with contextlib.ExitStack() as d2c:
            d2sb = lambda n, sh, d: d2c.enter_context(nc.sbuf_tensor(n, sh, d))
            hall = [d2sb(f"hall{i}", [128, E], F32) for i in range(2)]
            lexf = d2sb("lexf", [128, 128], F32)
            onescol_f = d2sb("onescol_f", [128, 1], F32)
            tot0 = d2sb("tot0", [1, E], F32)
            basev = d2sb("basev", [128, E], F32)
            PF = d2sb("PF", [128, E], F32)
            PFE_s = d2sb("PFE_s", [128, E], F32)
            PFE_g = d2sb("PFE_g", [128, E], F32)
            pen = d2sb("pen", [128, E], F32)
            ohk = d2sb("ohk", [128, E], F32)
            prodb = d2sb("prodb", [128, E], F32)
            slotf = d2sb("slotf", [128, 2], F32)
            dumpv = d2sb("dumpv", [128, 1], F32)
            vmask = d2sb("vmask", [128, 1], I32)
            sloti = d2sb("sloti", [128, TOPK], I32)
            tids = d2sb("tids", [128, 1], F32)
            sel32 = d2sb("sel32_sb", [NT, NT * 128], F32)
            psB = d2c.enter_context(nc.psum_tensor("psB", [128, E], F32))
            psBC = d2c.enter_context(nc.psum_tensor("psBC", [128, E], F32))
            s32v = P.dma("sync", lambda e: e.dma_start(out=sel32[:], in_=cin["sel32"][:, :]))

            agh = P.on("gpsimd", lambda e: e.collective_compute(
                "AllGather", ALU.bypass, rg, [hist_d[:, :].opt()], [hist_all_d[:, :].opt()]),
                wait=[P.t("hist_out")], inc="cc")
            lexv = P.on("vector", lambda e: e.tensor_copy(lexf[:], cs["lexclT"][:]),
                        wait=[tok_const, ("sa", P.val("sa"))], inc="va")
            onev = P.on("vector", lambda e: e.memset(onescol_f[:], 1.0), inc="va")
            for i in range(2):
                P.dma("sync", lambda e, i=i: e.dma_start(
                    out=hall[i][:], in_=hist_all_d[128 * i:128 * (i + 1), :]), wait=[agh])
            hav = ("dS", P.val("dS"))
            P.on("tensor", lambda e: e.matmul(psB[:], lexf[:], hall[0][:], start=True, stop=True),
                 wait=[hav, lexv], inc="tm")
            b0 = P.on("vector", lambda e: e.tensor_copy(basev[:], psB[:]),
                      wait=[("tm", P.val("tm"))], inc="va")
            d0 = P.dma("sync", lambda e: e.dma_start(out=base_d[0:128, :], in_=basev[:]), wait=[b0])
            P.on("tensor", lambda e: e.matmul(psBC[0:1, :], onescol_f[:], hall[0][:],
                                              start=True, stop=True), wait=[onev, hav], inc="tm")
            t0v = P.on("vector", lambda e: e.tensor_copy(tot0[:], psBC[0:1, :]),
                       wait=[("tm", P.val("tm"))], inc="va")
            P.on("tensor", lambda e: e.matmul(psB[:], lexf[:], hall[1][:], start=True, stop=False),
                 wait=[b0], inc="tm")
            P.on("tensor", lambda e: e.matmul(psB[:], cs["ones_row_f"][:, :], tot0[:],
                                              start=False, stop=True),
                 wait=[t0v], inc="tm")
            b1 = P.on("vector", lambda e: e.tensor_copy(basev[:], psB[:]),
                      wait=[("tm", P.val("tm")), d0], inc="va")
            P.dma("sync", lambda e: e.dma_start(out=base_d[128:256, :], in_=basev[:]), wait=[b1])
            bdv = ("dS", P.val("dS"))
            bgx = d2sb("bgx", [NT, 1], I32)
            base_own = d2sb("base_own", [NT, E], F32)
            P.dma("sync", lambda e: e.dma_start(out=bgx[:], in_=cin["base_gidx"][:, :]))
            bgv = ("dS", P.val("dS"))
            P.dma("gpsimd", lambda e: e.indirect_dma_start(
                out=base_own[:, :], out_offset=None,
                in_=base_d[:, :], in_offset=IOA(bgx[:, :], 0)),
                wait=[bdv, bgv])
            brv = ("dG", P.val("dG"))

            for tt in range(NT):
                P.on("tensor", lambda e, tt=tt: e.matmul(
                    psBC[:], sel32[:, 128 * tt:128 * (tt + 1)], base_own[:, :],
                    start=True, stop=True),
                    wait=[brv, s32v, P.t("psBC_free")], inc="tm")
                bcv = ("tm", P.val("tm"))
                VC(lambda e, tt=tt: e.tensor_tensor(
                    PF[:], psBC[:], P_all[:, E * tt:E * (tt + 1)], ALU.add),
                    wait=[bcv], tok="psBC_free")
                VC(lambda e: e.tensor_single_scalar(pen[:], PF[:], float(CAP), ALU.is_ge))
                VC(lambda e: e.tensor_add(PFE_s[:], PF[:], cs["e1024"][:]))
                VC(lambda e: e.scalar_tensor_tensor(
                    PFE_s[:], pen[:], BIG, PFE_s[:], ALU.mult, ALU.add))
                VC(lambda e: e.tensor_add(PFE_g[:], PF[:], cs["gcomb"][:]))
                VC(lambda e: e.scalar_tensor_tensor(
                    PFE_g[:], pen[:], BIG, PFE_g[:], ALU.mult, ALU.add))
                VC(lambda e, tt=tt: e.tensor_scalar(
                    tids[:], cs["iota_p"][:], cs["core_base"][:, 0:1], float(128 * tt),
                    ALU.add, ALU.add), wait=[P.t("tids_free")])
                for k in range(TOPK):
                    VC(lambda e, tt=tt, k=k: e.tensor_single_scalar(
                        ohk[:], scores_all[:, E * tt:E * (tt + 1)],
                        mx8_all[:, 8 * tt + k:8 * tt + k + 1], ALU.is_equal))
                    VC(lambda e: e.tensor_mul(prodb[:], ohk[:], PFE_s[:]))
                    VC(lambda e: e.reduce_sum(slotf[:, 0:1], prodb[:], axis=AX.X))
                    VC(lambda e: e.tensor_single_scalar(
                        slotf[:, 1:2], slotf[:, 0:1], 65536.0, ALU.is_lt))
                    VC(lambda e, tt=tt, k=k: e.tensor_mul(
                        wfin_all[:, TOPK * tt + k:TOPK * tt + k + 1],
                        wvec_all[:, TOPK * tt + k:TOPK * tt + k + 1], slotf[:, 1:2]))
                    VC(lambda e, tt=tt, k=k: e.tensor_scalar_add(
                        dumpv[:], cs["iota_p"][:], float(E * CAP + 128 * (TOPK * tt + k))))
                    VC(lambda e: e.tensor_single_scalar(
                        vmask[:], slotf[:, 0:1], 65536.0, ALU.is_lt))
                    VC(lambda e: e.select(
                        out=dumpv[:], mask=vmask[:], on_true=slotf[:, 0:1],
                        on_false=dumpv[:]))
                    sv = VC(lambda e, k=k: e.tensor_copy(sloti[:, k:k + 1], dumpv[:]),
                            wait=[P.t("sloti_free")] if k == 0 else [])
                    VC(lambda e: e.tensor_mul(prodb[:], ohk[:], PFE_g[:]))
                    VC(lambda e: e.reduce_sum(slotf[:, 1:2], prodb[:], axis=AX.X))
                    VC(lambda e, tt=tt, k=k: e.tensor_scalar_min(
                        gidx_all[:, TOPK * tt + k:TOPK * tt + k + 1], slotf[:, 1:2],
                        float(E * CAP - 1)))
                    P.dma("gpsimd", lambda e, k=k: e.indirect_dma_start(
                        out=slot_tok_d[:, :], out_offset=IOA(sloti[:, k:k + 1], 1),
                        in_=tids[:, :], in_offset=None),
                        wait=[sv, P.t("slotz")] if k == 0 else [sv])
                P.tok["sloti_free"] = ("dG", P.val("dG"))
                P.tok["tids_free"] = ("dG", P.val("dG"))

            nsl = E * CAP // 128
            slsum = d2sb("slsum", [128, nsl // 128, 128], F32)
            slpart = d2sb("slpart", [128, nsl // 128, 128], F32)
            sgx = d2sb("sgx", [64, 1], I32)
            slmine = d2sb("slmine", [64, 128], F32)
            cpv = P.dma("sync", lambda e: e.dma_start(
                out=slot_rs_d[:, :], in_=slot_tok_d[0:nsl, :]),
                wait=[("dG", P.val("dG"))])
            agsl = P.on("gpsimd", lambda e: e.collective_compute(
                "AllGather", ALU.bypass, rg, [slot_rs_d[:, :].opt()],
                [slot_all8_d[:, :].opt()]),
                wait=[cpv], inc="cc")
            l0 = P.dma("sync", lambda e: e.dma_start(
                out=slsum[:], in_=slot_all8_d[0:nsl, :].rearrange("(a p) c -> p a c", p=128)),
                wait=[agsl])
            addtoks = [l0]
            for rr in range(1, NCORES):
                lr = P.dma("sync", lambda e, rr=rr: e.dma_start(
                    out=slpart[:],
                    in_=slot_all8_d[nsl * rr:nsl * (rr + 1), :].rearrange(
                        "(a p) c -> p a c", p=128)),
                    wait=[P.t("slpart_used")] if rr >= 2 else [])
                VC(lambda e: e.tensor_add(slsum[:], slsum[:], slpart[:]),
                   wait=[lr] + ([l0] if rr == 1 else []), tok="slpart_used")
            ssv = P.dma("sync", lambda e: e.dma_start(
                out=slot_sum_d[:, :].rearrange("(a p) c -> p a c", p=128), in_=slsum[:]),
                wait=[P.t("slpart_used")])
            sgv = P.dma("sync", lambda e: e.dma_start(out=sgx[:], in_=cin["slot_gidx"][:, :]))
            gmv = P.dma("gpsimd", lambda e: e.indirect_dma_start(
                out=slmine[:, :], out_offset=None,
                in_=slot_sum_d[:, :], in_offset=IOA(sgx[:, :], 0)),
                wait=[ssv, sgv])
            rsv = P.dma("sync", lambda e: e.dma_start(out=slot_own_d[:, :], in_=slmine[:]),
                        wait=[gmv])

        if "scores" in dbg:
            P.dma("gpsimd", lambda e: e.dma_start(out=dbg["scores"][:, :], in_=scores_all[:]))
        if "slot_tok" in dbg:
            P.dma("gpsimd", lambda e: e.dma_start(out=dbg["slot_tok"][:, :].rearrange("(a b) o -> a (b o)", b=128), in_=slot_tok_d[0:E * CAP // 128, :]),
                  wait=[rsv])
        if "wfin" in dbg:
            P.dma("gpsimd", lambda e: e.dma_start(out=dbg["wfin"][:, :], in_=wfin_all[:]))
        if "gidx" in dbg:
            P.dma("gpsimd", lambda e: e.dma_start(out=dbg["gidx"][:, :], in_=gidx_all[:]))
        if "shared" in dbg:
            P.dma("gpsimd", lambda e: e.dma_start(out=dbg["shared"][:, :], in_=shared_d[:, :]))
        if "hist_all" in dbg:
            P.dma("gpsimd", lambda e: e.dma_start(out=dbg["hist_all"][:, :], in_=hist_all_d[:, :]))
        if "base" in dbg:
            P.dma("gpsimd", lambda e: e.dma_start(out=dbg["base"][:, :], in_=base_d[:, :]))

        with contextlib.ExitStack() as d3c:
            d3sb = lambda n, sh, d: d3c.enter_context(nc.sbuf_tensor(n, sh, d))
            slf = d3sb("slf", [128, 8], F32)
            sli = d3sb("sli", [128, 8], I32)
            xg2 = [d3sb(f"xg2{i}", [128, 8, D], BF16) for i in range(2)]
            xdT = [d3sb("xdT0", [128, ND, CAP], BF16)]
            wupb = [d3sb(f"wupb{i}", [128, ND, 2 * H_EXP], BF16) for i in range(2)]
            wdnb = [d3sb(f"wdnb{i}", [128, 2, D], BF16) for i in range(2)]
            hrT = d3sb("hrT", [128, 2, CAP], BF16)
            gpart = d3sb("gpart", [128, 2, CAP], BF16)
            odst = d3sb("odst", [128, D], BF16)
            psE = [d3c.enter_context(nc.psum_tensor(f"psE{i}", [128, 512], F32)) for i in range(2)]

            gavs = []
            for el in range(ELOC):
                par = el % 2
                xg = xg2[par]
                for j in range(8):
                    sl = P.dma("sync", lambda e, el=el, j=j: e.dma_start(
                        out=slf[:, j:j + 1],
                        in_=slot_own_d[8 * el + j:8 * el + j + 1, :]),
                        wait=[rsv, P.t("slf_free")] if j == 0 else [])
                sliv = P.on("vector", lambda e: e.tensor_copy(sli[:], slf[:]),
                            wait=[sl, P.t("sli_free")], inc="va", tok="slf_free")
                for j in range(8):
                    gj = P.dma("gpsimd", lambda e, j=j, xg=xg: e.indirect_dma_start(
                        out=xg[:, j, :], out_offset=None,
                        in_=ln2x_all_d[:, :], in_offset=IOA(sli[:, j:j + 1], 0)),
                        wait=[sliv, agl, P.t(f"xg{par}_free")] if j == 0 else [])
                    P.dma("sync", lambda e, el=el, j=j, xg=xg: e.dma_start(
                        out=xdisp_d[CAP * el + 128 * j:CAP * el + 128 * (j + 1), :],
                        in_=xg[:, j, :]), wait=[gj])
                gavs.append(("dS", P.val("dS")))
                P.tok["sli_free"] = ("dG", P.val("dG"))
                P.tok[f"xg{par}_free"] = ("dS", P.val("dS"))

            for el in range(ELOC):
                par = el % 2
                wub, wdb, xb = wupb[par], wdnb[par], xdT[0]
                wu = P.dma("gpsimd", lambda e, el=el, wub=wub: e.dma_start(
                    out=wub[:], in_=wup[el, :, :].rearrange("(dt p) o -> p dt o", p=128)),
                    wait=[P.t(f"wupb{par}_free")])
                wd = P.dma("gpsimd", lambda e, el=el, wdb=wdb: e.dma_start(
                    out=wdb[:], in_=wdn[el, :, :].rearrange("(ht p) o -> p ht o", p=128)),
                    wait=[P.t(f"wdnb{par}_free")])
                for ct in range(ND):
                    P.dma("scalar", lambda e, el=el, ct=ct, xb=xb: e.dma_start(
                        out=xb[:, ct, :],
                        in_=xdisp_d[CAP * el:CAP * (el + 1), 128 * ct:128 * (ct + 1)],
                        transpose=True),
                        wait=[gavs[el], P.t("xdT_free")] if ct == 0 else [])
                xtv = ("dA", P.val("dA"))
                for ht in range(4):
                    for sc in range(2):
                        pp = sc
                        for dtl in range(ND):
                            P.on("tensor", lambda e, wub=wub, xb=xb, pp=pp, dtl=dtl, ht=ht, sc=sc:
                                 e.matmul(
                                     psE[pp][:], wub[:, dtl, 128 * ht:128 * (ht + 1)],
                                     xb[:, dtl, 512 * sc:512 * (sc + 1)],
                                     start=(dtl == 0), stop=(dtl == ND - 1)),
                                 wait=[xtv, wu, P.t(f"psE{pp}_free")] if dtl == 0 else [],
                                 inc="tm" if dtl == ND - 1 else None)
                        umm = ("tm", P.val("tm"))
                        if ht < 2:
                            P.on("scalar", lambda e, pp=pp, ht=ht, sc=sc: e.activation(
                                out=gpart[:, ht, 512 * sc:512 * (sc + 1)], in_=psE[pp][:],
                                func=AF.Silu),
                                wait=[umm, P.t("gpart_free")] if (ht, sc) == (0, 0) else [umm],
                                inc="sa", tok=f"psE{pp}_free")
                        else:
                            P.on("scalar", lambda e, pp=pp, ht=ht, sc=sc: e.copy(
                                out=hrT[:, ht - 2, 512 * sc:512 * (sc + 1)], in_=psE[pp][:]),
                                wait=[umm, P.t("hrT_free")] if (ht, sc) == (2, 0) else [umm],
                                inc="sa", tok=f"psE{pp}_free")
                P.tok[f"wupb{par}_free"] = ("tm", P.val("tm"))
                P.tok["xdT_free"] = ("tm", P.val("tm"))
                hupv = ("sa", P.val("sa"))
                P.on("vector", lambda e: e.tensor_mul(hrT[:, 0, :], hrT[:, 0, :], gpart[:, 0, :]),
                     wait=[hupv])
                hv = P.on("vector", lambda e: e.tensor_mul(
                    hrT[:, 1, :], hrT[:, 1, :], gpart[:, 1, :]), inc="va", tok="gpart_free")
                for j in range(8):
                    for oc in range(4):
                        pp = (j * 4 + oc) % 2
                        for h2 in range(2):
                            P.on("tensor", lambda e, wdb=wdb, pp=pp, h2=h2, j=j, oc=oc: e.matmul(
                                psE[pp][:], hrT[:, h2, 128 * j:128 * (j + 1)],
                                wdb[:, h2, 512 * oc:512 * (oc + 1)],
                                start=(h2 == 0), stop=(h2 == 1)),
                                wait=[hv, wd, P.t(f"psE{pp}_free")] if h2 == 0 else [],
                                inc="tm" if h2 == 1 else None)
                        dmm = ("tm", P.val("tm"))
                        P.on("scalar", lambda e, pp=pp, oc=oc: e.copy(
                            out=odst[:, 512 * oc:512 * (oc + 1)], in_=psE[pp][:]),
                            wait=[dmm, P.t("odst_free")] if oc == 0 else [dmm],
                            inc="sa", tok=f"psE{pp}_free")
                    ov = ("sa", P.val("sa"))
                    P.dma("sync", lambda e, el=el, j=j: e.dma_start(
                        out=odisp_d[CAP * el + 128 * j:CAP * el + 128 * (j + 1), :], in_=odst[:]),
                        wait=[ov], tok="odst_free")
                P.tok[f"wdnb{par}_free"] = ("tm", P.val("tm"))
                P.tok["hrT_free"] = ("tm", P.val("tm"))
                if el == 3:
                    P.on("gpsimd", lambda e: e.collective_compute(
                        "AllGather", ALU.bypass, rg, [odisp_d[0:4 * CAP, :].opt()],
                        [odall_d[0:NCORES * 4 * CAP, :].opt()]),
                        wait=[("dS", P.val("dS"))], inc="cc")

        if "odisp" in dbg:
            P.dma("gpsimd", lambda e: e.dma_start(out=dbg["odisp"][:, :], in_=odisp_d[:, :]),
                  wait=[("dS", P.val("dS"))])

        ago = P.on("gpsimd", lambda e: e.collective_compute(
            "AllGather", ALU.bypass, rg, [odisp_d[4 * CAP:8 * CAP, :].opt()],
            [odall_d[NCORES * 4 * CAP:, :].opt()]),
            wait=[("dS", P.val("dS"))], inc="cc")

        with contextlib.ExitStack() as d4c:
            d4sb = lambda n, sh, d: d4c.enter_context(nc.sbuf_tensor(n, sh, d))
            gt = [d4sb(f"gt{i}", [128, D], BF16) for i in range(6)]
            acc = d4sb("acc", [128, D], BF16)
            acc2 = d4sb("acc2", [128, D], BF16)
            sht = d4sb("sht", [128, D], BF16)
            x2r = [d4sb(f"x2r{i}", [128, D], F32) for i in range(2)]
            ffo = d4sb("ffo", [128, D], F32)
            yst = d4sb("yst", [128, D], F32)
            st4 = d4sb("st4", [128, 12], F32)
            sqs4 = d4sb("sqs4", [128, 512], F32)

            for i in range(6):
                P.on("vector", lambda e, i=i: e.memset(gt[i][:], 0.0), inc="va",
                     tok=f"gt{i}_init")

            for tt in range(NT):
                xpar = tt % 2
                xb = x2r[xpar]
                xr = P.dma("sync", lambda e, tt=tt, xb=xb: e.dma_start(
                    out=xb[:], in_=x2_d[128 * tt:128 * (tt + 1), :]),
                    wait=[P.t(f"x2r{xpar}_free")])
                sh = P.dma("sync", lambda e, tt=tt: e.dma_start(
                    out=sht[:], in_=shared_d[128 * tt:128 * (tt + 1), :]),
                    wait=[P.t("sht_free")] + ([("dS", P.val("dS"))] if tt == 0 else []))
                gvs = {}

                def emit_gather(k, tt=tt):
                    gp = k % 6
                    gb = gt[gp]
                    gvs[k] = P.dma("gpsimd", lambda e, gb=gb, tt=tt, k=k: e.indirect_dma_start(
                        out=gb[:, :], out_offset=None,
                        in_=odall_d[:, :],
                        in_offset=IOA(gidx_all[:, TOPK * tt + k:TOPK * tt + k + 1], 0)),
                        wait=[ago, P.t(f"gt{gp}_free"), P.t(f"gt{gp}_init")])

                for k in range(TOPK):
                    emit_gather(k)
                for k in range(TOPK):
                    gp = k % 6
                    gb = gt[gp]
                    if k == 0:
                        av = VC(lambda e, tt=tt, k=k, gb=gb: e.tensor_scalar_mul(
                            acc[:], gb[:], wfin_all[:, TOPK * tt + k:TOPK * tt + k + 1]),
                            wait=[gvs[k], P.t("acc_free")])
                    else:
                        av = VC(lambda e, tt=tt, k=k, gb=gb: e.scalar_tensor_tensor(
                            acc[:], gb[:], wfin_all[:, TOPK * tt + k:TOPK * tt + k + 1],
                            acc[:], ALU.mult, ALU.add), wait=[gvs[k]])
                    P.tok[f"gt{gp}_free"] = av
                fv = VC(lambda e: e.tensor_add(ffo[:], acc[:], sht[:]),
                        wait=[sh, P.t("ffo_sq")], tok="acc_free")
                P.tok["sht_free"] = fv
                VC(lambda e: e.reduce_sum(st4[:, 0:1], ffo[:], axis=AX.X), wait=[fv])
                for oc in range(4):
                    P.on("scalar", lambda e, oc=oc: e.activation(
                        out=sqs4[:], in_=ffo[:, 512 * oc:512 * (oc + 1)], func=AF.Square,
                        accum_out=st4[:, 4 + oc:5 + oc]),
                        wait=[fv] if oc == 0 else [], inc="sa" if oc == 3 else None,
                        tok="ffo_sq" if oc == 3 else None)
                sq4 = ("sa", P.val("sa"))
                VC(lambda e: e.reduce_sum(st4[:, 1:2], st4[:, 4:8], axis=AX.X), wait=[sq4])
                VC(lambda e: e.tensor_scalar_mul(st4[:, 0:1], st4[:, 0:1], 1.0 / D))
                VC(lambda e: e.tensor_scalar_mul(st4[:, 1:2], st4[:, 1:2], 1.0 / D))
                VC(lambda e: e.tensor_mul(st4[:, 2:3], st4[:, 0:1], st4[:, 0:1]))
                VC(lambda e: e.tensor_sub(st4[:, 1:2], st4[:, 1:2], st4[:, 2:3]))
                VC(lambda e: e.tensor_scalar_add(st4[:, 1:2], st4[:, 1:2], LN_EPS), tok="var4")
                P.on("scalar", lambda e: e.activation(out=st4[:, 2:3], in_=st4[:, 1:2], func=AF.Sqrt),
                     wait=[P.t("var4")], inc="sa", tok="sqrt4")
                VC(lambda e: e.reciprocal(st4[:, 3:4], st4[:, 2:3]), wait=[P.t("sqrt4")])
                VC(lambda e: e.tensor_scalar(
                    yst[:], ffo[:], st4[:, 0:1], st4[:, 3:4], ALU.subtract, ALU.mult),
                    wait=[P.t("yst_free")])
                yv = VC(lambda e, xb=xb: e.tensor_add(yst[:], yst[:], xb[:]),
                        wait=[xr], tok=f"x2r{xpar}_free")
                P.dma("sync", lambda e, tt=tt: e.dma_start(
                    out=y_out[128 * tt:128 * (tt + 1), :], in_=yst[:]), wait=[yv], tok="yst_free")

    fin = [("dS", P.val("dS")), ("dG", P.val("dG")), ("dA", P.val("dA"))]
    P.on("sync", fn=None, wait=fin)
    P.emit()
    return nc, ctx


def kernel(L0_attn_x, L0_attn_Wq, L0_attn_Wk, L0_attn_Wv, L0_attn_Wo,
           L0_ffn_w_gate, L0_ffn_w_shared_up, L0_ffn_w_shared_down,
           L0_ffn_w_up, L0_ffn_w_down, _cfg=None):
    cfg = dict(_cfg or {})
    nc, ctx = build_program(cfg)

    wqT = np.ascontiguousarray(np.asarray(L0_attn_Wq).T)
    wkT = np.ascontiguousarray(np.asarray(L0_attn_Wk).T)
    wvT = np.ascontiguousarray(np.asarray(L0_attn_Wv).T)
    woT = np.ascontiguousarray(np.asarray(L0_attn_Wo).T)
    x = np.asarray(L0_attn_x)
    wup = np.asarray(L0_ffn_w_up)
    wdn = np.asarray(L0_ffn_w_down)

    in_maps = []
    for c in range(NCORES):
        m = {
            "x": np.ascontiguousarray(x[c]),
            "wqT": wqT, "wkT": wkT, "wvT": wvT, "woT": woT,
            "wgate": np.ascontiguousarray(np.asarray(L0_ffn_w_gate)),
            "wsu": np.ascontiguousarray(np.asarray(L0_ffn_w_shared_up)),
            "wsd": np.ascontiguousarray(np.asarray(L0_ffn_w_shared_down)),
            "wup": np.ascontiguousarray(wup[ELOC * c:ELOC * (c + 1)]),
            "wdn": np.ascontiguousarray(wdn[ELOC * c:ELOC * (c + 1)]),
        }
        m.update(make_host_constants(c))
        in_maps.append(m)

    res = run_bass_kernel_spmd(nc, in_maps, core_ids=list(range(NCORES)))
    ctx.close()
    out = np.stack([res.results[c]["y"] for c in range(NCORES)], axis=0)
    if cfg.get("return_raw"):
        return out, res
    return out


# version 16
# speedup vs baseline: 1.0502x; 1.0502x over previous
import contextlib
import numpy as np
import ml_dtypes
import concourse.bass as bass
import concourse.mybir as mybir
from concourse.bass_utils import run_bass_kernel_spmd

F32 = mybir.dt.float32
BF16 = mybir.dt.bfloat16
I32 = mybir.dt.int32
AF = mybir.ActivationFunctionType
ALU = mybir.AluOpType
AX = mybir.AxisListType
IOA = bass.IndirectOffsetOnAxis

NCORES = 8
D = 2048
T = 4096
NH = 16
HD = 128
E = 64
ELOC = 8
CAP = 1024
TOPK = 6
HS = 512
H_EXP = 256
LN_EPS = 1e-5
ATT_EPS = 1e-6
BIG = 4.0e6

NT = T // 128
NCH = T // 512
ND = D // 128


class Prog:
    ENGINES = ("sync", "tensor", "vector", "scalar", "gpsimd")

    def __init__(self, nc, ctx):
        self.nc = nc
        self.ops = {e: [] for e in self.ENGINES}
        self.sem_val = {}
        self.sems = {}
        self.ctx = ctx
        self.tok = {}

    def sem(self, name):
        if name not in self.sems:
            self.sems[name] = self.ctx.enter_context(self.nc.semaphore(name))
            self.sem_val[name] = 0
        return self.sems[name]

    def val(self, name):
        self.sem(name)
        return self.sem_val[name]

    def on(self, engine, fn=None, wait=(), inc=None, inc_amt=1, tok=None):
        wl = []
        for item in wait:
            if item is None:
                continue
            s, v = item
            self.sem(s)
            assert v <= self.sem_val[s], f"wait on future: {s} {v} > {self.sem_val[s]}"
            wl.append((s, v))
        ret = None
        if inc is not None:
            self.sem(inc)
            self.sem_val[inc] += inc_amt
            ret = (inc, self.sem_val[inc])
        self.ops[engine].append((fn, tuple(wl), inc, inc_amt))
        if tok is not None:
            assert ret is not None
            self.tok[tok] = ret
        return ret

    def dma(self, engine, fn, wait=(), tok=None):
        q = {"sync": "dS", "scalar": "dA", "gpsimd": "dG", "vector": "dV"}[engine]
        return self.on(engine, fn, wait=wait, inc=q, inc_amt=16, tok=tok)

    def t(self, name):
        return self.tok.get(name)

    def barrier(self):
        bar = "bar"
        self.sem(bar)
        qwaits = {"sync": "dS", "scalar": "dA", "gpsimd": "dG"}
        for e in self.ENGINES:
            w = ()
            if e in qwaits:
                q = qwaits[e]
                self.sem(q)
                w = ((q, self.sem_val[q]),)
            self.ops[e].append((None, w, bar, 1))
        self.sem_val[bar] += len(self.ENGINES)
        v = self.sem_val[bar]
        for e in self.ENGINES:
            self.ops[e].append((None, ((bar, v),), None, 1))

    def emit(self):
        nc = self.nc
        sems = self.sems
        with nc.Block() as block:
            for ename in self.ENGINES:
                ops = self.ops[ename]

                def mk(eng, ops=ops):
                    for fn, waits, inc, inc_amt in ops:
                        for (w, v) in waits:
                            eng.wait_ge(sems[w], v)
                        if fn is None:
                            if inc is not None:
                                eng.nop().then_inc(sems[inc], inc_amt)
                            continue
                        inst = fn(eng)
                        if inc is not None:
                            inst.then_inc(sems[inc], inc_amt)
                getattr(block, ename)(mk)


def make_host_constants(core):
    c = {}
    j = np.arange(64, dtype=np.float64)
    theta = 10000.0 ** (-j / 64.0)
    pos = np.arange(T, dtype=np.float64)
    ang = pos[None, :] * theta[:, None]
    cos = np.cos(ang)
    sin = np.sin(ang)
    c["rope_c"] = np.concatenate([cos, cos], 0).astype(ml_dtypes.bfloat16)
    c["rope_s"] = np.concatenate([-sin, sin], 0).astype(ml_dtypes.bfloat16)
    c["ident"] = np.eye(128, dtype=np.float32).astype(ml_dtypes.bfloat16)
    c["lexclT"] = np.triu(np.ones((128, 128), np.float32), 1).astype(ml_dtypes.bfloat16)
    for r in range(4):
        m = (128 * r + np.arange(128)[:, None] <= np.arange(512)[None, :])
        c[f"mask{r}"] = m.astype(ml_dtypes.bfloat16)
    c["ones_col"] = np.ones((128, 1), np.float32).astype(ml_dtypes.bfloat16)
    c["ones_row_f"] = np.ones((1, 128), np.float32)
    c["iota_p"] = np.arange(128, dtype=np.float32).reshape(128, 1)
    e = np.arange(E, dtype=np.float32)
    c["e1024"] = np.tile((e * CAP)[None, :], (128, 1)).astype(np.float32)
    c["core_base"] = np.full((128, 1), core * T, np.float32)
    c["base_gidx"] = (core * NT + np.arange(NT, dtype=np.int32)).reshape(NT, 1)
    c["slot_gidx"] = (core * 64 + np.arange(64, dtype=np.int32)).reshape(64, 1)
    ee = np.arange(E)
    gg = (ee // 8) * 4096 + (ee % 8 % 4) * 1024 + np.where(ee % 8 >= 4, 32768, 0)
    c["gcomb"] = np.tile(gg[None, :].astype(np.float32), (128, 1))
    sel = np.zeros((NT, NT * 128), np.float32)
    for t in range(NT):
        sel[t, 128 * t:128 * (t + 1)] = 1.0
    c["sel32"] = sel
    return c


CONST_SPECS = {
    "rope_c": ([128, T], BF16), "rope_s": ([128, T], BF16),
    "ident": ([128, 128], BF16), "lexclT": ([128, 128], BF16),
    "mask0": ([128, 512], BF16), "mask1": ([128, 512], BF16),
    "mask2": ([128, 512], BF16), "mask3": ([128, 512], BF16),
    "ones_col": ([128, 1], BF16), "ones_row_f": ([1, 128], F32),
    "iota_p": ([128, 1], F32), "e1024": ([128, E], F32),
    "core_base": ([128, 1], F32), "base_gidx": ([NT, 1], I32),
    "slot_gidx": ([64, 1], I32), "gcomb": ([128, E], F32),
    "sel32": ([NT, NT * 128], F32),
}


def build_program(cfg):
    nc = bass.Bass("TRN2", target_bir_lowering=False)
    ctx = contextlib.ExitStack()
    heads = cfg.get("heads", NH)
    debug = cfg.get("debug_outs", ())
    upto = cfg.get("upto", "Z")
    with_ffn = cfg.get("with_ffn", True)
    with_attn = cfg.get("with_attn", True)

    dt = nc.dram_tensor
    x_in = dt("x", [T, D], F32, kind="ExternalInput").ap()
    wqT = dt("wqT", [D, D], F32, kind="ExternalInput").ap()
    wkT = dt("wkT", [D, D], F32, kind="ExternalInput").ap()
    wvT = dt("wvT", [D, D], F32, kind="ExternalInput").ap()
    woT = dt("woT", [D, D], F32, kind="ExternalInput").ap()
    wgate = dt("wgate", [D, E], F32, kind="ExternalInput").ap()
    wsu = dt("wsu", [D, 2 * HS], F32, kind="ExternalInput").ap()
    wsd = dt("wsd", [HS, D], F32, kind="ExternalInput").ap()
    wup = dt("wup", [ELOC, D, 2 * H_EXP], F32, kind="ExternalInput").ap()
    wdn = dt("wdn", [ELOC, H_EXP, D], F32, kind="ExternalInput").ap()
    y_out = dt("y", [T, D], F32, kind="ExternalOutput").ap()
    cin = {k: dt(k, list(sh), d, kind="ExternalInput").ap() for k, (sh, d) in CONST_SPECS.items()}

    lnx_d = dt("lnx_d", [T, D], BF16).ap()
    qT_d = dt("qT_d", [NH, 128, T], BF16).ap()
    kT_d = dt("kT_d", [NH, 128, T], BF16).ap()
    v_d = dt("v_d", [NH, 128, NT, HD], BF16).ap()
    ctx_d = dt("ctx_d", [NH, T, HD], BF16).ap()
    x2_d = dt("x2_d", [T, D], F32).ap()
    shared_d = dt("shared_d", [T, D], BF16).ap()
    ln2x_d = dt("ln2x_d", [T, D], BF16).ap()
    hist_d = dt("hist_d", [NT, E], F32).ap()
    hist_all_d = dt("hist_all_d", [NCORES * NT, E], F32).ap()
    base_d = dt("base_d", [NCORES * NT, E], F32).ap()
    slot_tok_d = dt("slot_tok_d", [E * CAP // 128 + NT * TOPK + 1, 128], F32).ap()
    slot_rs_d = dt("slot_rs_d", [E * CAP // 128, 128], F32).ap()
    slot_all8_d = dt("slot_all8_d", [NCORES * E * CAP // 128, 128], F32).ap()
    slot_sum_d = dt("slot_sum_d", [E * CAP // 128, 128], F32).ap()
    slot_own_d = dt("slot_own_d", [ELOC * CAP // 128, 128], F32).ap()
    ln2x_all_d = dt("ln2x_all_d", [NCORES * T, D], BF16, addr_space="Shared").ap()
    xdisp_d = dt("xdisp_d", [ELOC * CAP, D], BF16).ap()
    odisp_d = dt("odisp_d", [ELOC * CAP, D], BF16).ap()
    odall_d = dt("odall_d", [E * CAP, D], BF16, addr_space="Shared").ap()

    DBG_SHAPES = {"x2": ([T, D], F32), "lnx": ([T, D], BF16),
                  "scores": ([128, NT * E], F32), "ln2x": ([T, D], BF16),
                  "shared": ([T, D], BF16), "slot_tok": ([E * CAP, 1], F32),
                  "odisp": ([ELOC * CAP, D], BF16), "wfin": ([128, NT * TOPK], F32),
                  "gidx": ([128, NT * TOPK], I32), "hist_all": ([NCORES * NT, E], F32),
                  "base": ([NCORES * NT, E], F32), "qt": ([128, T], BF16),
                  "kt": ([128, T], BF16), "vt": ([128, NT * HD], BF16),
                  "ctxt": ([128, 512], BF16)}
    dbg = {}
    for name in debug:
        sh, d = DBG_SHAPES[name]
        dbg[name] = dt("dbg_" + name, sh, d, kind="ExternalOutput").ap()

    P = Prog(nc, ctx)
    rg = [list(range(NCORES))]

    def VC(fn, wait=(), tok=None):
        w = [x for x in wait if x is not None] + [("va", P.val("va"))]
        return P.on("vector", fn, wait=w, inc="va", tok=tok)

    cs = {}
    for k, (sh, d) in CONST_SPECS.items():
        if k in ("rope_c", "rope_s", "sel32"):
            continue
        cs[k] = ctx.enter_context(nc.sbuf_tensor("c_" + k, list(sh), d))
        P.dma("sync", lambda e, a=cs[k], b=cin[k]: e.dma_start(out=a[:], in_=b[:, :]))
    tok_const = ("dS", P.val("dS"))

    if with_attn:
      with contextlib.ExitStack() as actx:
        asb = lambda n, sh, d: actx.enter_context(nc.sbuf_tensor(n, sh, d))
        rope_c = asb("rope_c_sb", [128, T], BF16)
        rope_s = asb("rope_s_sb", [128, T], BF16)
        P.dma("sync", lambda e: e.dma_start(out=rope_c[:], in_=cin["rope_c"][:, :]))
        P.dma("sync", lambda e: e.dma_start(out=rope_s[:], in_=cin["rope_s"][:, :]), tok="ropeld")

        xt = [asb(f"xt{i}", [128, D], F32) for i in range(2)]
        lnxt = [asb(f"lnxt{i}", [128, D], BF16) for i in range(2)]
        stat = asb("stat", [128, 8], F32)
        sq_scr = asb("sq_scr", [128, D], F32)
        lnxT = asb("lnxT", [128, ND, 2048], BF16)
        wsb = [asb(f"wsb{i}", [128, ND, 1024], BF16) for i in range(2)]
        qs_s = asb("qs_s", [128, 512], BF16)
        qev = asb("qev", [128, 512], BF16)
        t1b = asb("t1b", [128, 512], BF16)
        rot = asb("rot", [128, 512], BF16)
        pj = [actx.enter_context(nc.psum_tensor(f"pj{i}", [128, 512], F32)) for i in range(2)]

        for tt in range(NT):
            par = tt % 2
            b = xt[par]
            ld = P.dma("sync", lambda e, b=b, tt=tt: e.dma_start(
                out=b[:], in_=x_in[128 * tt:128 * (tt + 1), :]),
                wait=[P.t(f"xt{par}_free")], tok=f"xt{par}_rdy")
            P.on("scalar", lambda e, b=b: e.activation(
                out=sq_scr[:], in_=b[:], func=AF.Square, accum_out=stat[:, 1:2]),
                wait=[ld, P.t("stat1_free")], inc="sa")
            sa_sq = ("sa", P.val("sa"))
            VC(lambda e, b=b: e.reduce_sum(stat[:, 0:1], b[:], axis=AX.X), wait=[ld])
            VC(lambda e: e.tensor_scalar_mul(stat[:, 2:3], stat[:, 0:1], 1.0 / D))
            VC(lambda e: e.tensor_scalar_mul(stat[:, 3:4], stat[:, 1:2], 1.0 / D),
               wait=[sa_sq], tok="stat1_free")
            VC(lambda e: e.tensor_mul(stat[:, 4:5], stat[:, 2:3], stat[:, 2:3]))
            VC(lambda e: e.tensor_sub(stat[:, 3:4], stat[:, 3:4], stat[:, 4:5]))
            VC(lambda e: e.tensor_scalar_add(stat[:, 3:4], stat[:, 3:4], LN_EPS), tok="var_rdy")
            P.on("scalar", lambda e: e.activation(out=stat[:, 5:6], in_=stat[:, 3:4], func=AF.Sqrt),
                 wait=[P.t("var_rdy")], inc="sa", tok="sqrt_rdy")
            VC(lambda e: e.reciprocal(stat[:, 6:7], stat[:, 5:6]), wait=[P.t("sqrt_rdy")])
            lno = VC(lambda e, b=b, o=lnxt[par]: e.tensor_scalar(
                o[:], b[:], stat[:, 2:3], stat[:, 6:7], ALU.subtract, ALU.mult),
                wait=[P.t(f"lnxt{par}_free")], tok=f"xt{par}_free")
            P.dma("sync", lambda e, o=lnxt[par], tt=tt: e.dma_start(
                out=lnx_d[128 * tt:128 * (tt + 1), :], in_=o[:]),
                wait=[lno], tok=f"lnxt{par}_free")
            if tt == NT // 2 - 1:
                P.tok["lnx_half0"] = ("dS", P.val("dS"))
        lnx_all = ("dS", P.val("dS"))

        def load_whalf(par, wT, oh):
            return P.dma("gpsimd", lambda e, wT=wT, oh=oh, par=par: e.dma_start(
                out=wsb[par][:],
                in_=wT[:, 1024 * oh:1024 * (oh + 1)].rearrange("(dt p) o -> p dt o", p=128)),
                wait=[P.t(f"wsb{par}_free")], tok=f"wsb{par}_rdy")

        jobs = []
        for wi, (wT, outd) in enumerate([(wqT, qT_d), (wkT, kT_d)]):
            jobs.append(("qk", wT, outd, 0))
            jobs.append(("qk", wT, outd, 1))
        jobs.append(("v", wvT, None, 0))
        jobs.append(("v", wvT, None, 1))

        for half in range(2):
            t0 = 2048 * half
            for ct in range(ND):
                P.dma("scalar", lambda e, ct=ct, t0=t0: e.dma_start(
                    out=lnxT[:, ct, :], in_=lnx_d[t0:t0 + 2048, 128 * ct:128 * (ct + 1)],
                    transpose=True),
                    wait=[P.t("lnx_half0") if half == 0 else lnx_all,
                          P.t("lnxT_free")] if ct == 0 else [])
            ltdone = ("dA", P.val("dA"))

            for ji, (kind, wT, outd, oh) in enumerate(jobs):
                par = ji % 2
                load_whalf(par, wT, oh)
                wrdy = P.t(f"wsb{par}_rdy")
                if kind == "qk":
                    for oo in range(8):
                        o = 8 * oh + oo
                        for c4 in range(4):
                            pp = c4 % 2
                            for dtl in range(ND):
                                P.on("tensor", lambda e, par=par, pp=pp, dtl=dtl, c4=c4, oo=oo:
                                     e.matmul(
                                         pj[pp][:], wsb[par][:, dtl, 128 * oo:128 * (oo + 1)],
                                         lnxT[:, dtl, 512 * c4:512 * (c4 + 1)],
                                         start=(dtl == 0), stop=(dtl == ND - 1)),
                                     wait=[wrdy, ltdone, P.t(f"pj{pp}_free")] if dtl == 0 else [],
                                     inc="tm" if dtl == ND - 1 else None)
                            mmv = ("tm", P.val("tm"))
                            tg = t0 + 512 * c4
                            P.on("scalar", lambda e, pp=pp: e.copy(out=qev[:], in_=pj[pp][:]),
                                 wait=[mmv, P.t("qev_free")], inc="sa", tok=f"pj{pp}_free")
                            ev = ("sa", P.val("sa"))
                            VC(lambda e: e.tensor_copy(qs_s[0:64, :], qev[64:128, :]),
                               wait=[ev, P.t("ropeld"), P.t("rot_free")])
                            VC(lambda e: e.tensor_copy(qs_s[64:128, :], qev[0:64, :]))
                            VC(lambda e, tg=tg: e.tensor_mul(
                                t1b[:], qev[:], rope_c[:, tg:tg + 512]), tok="qev_free")
                            VC(lambda e, tg=tg: e.tensor_mul(
                                rot[:], qs_s[:], rope_s[:, tg:tg + 512]))
                            ro = VC(lambda e: e.tensor_add(rot[:], rot[:], t1b[:]))
                            P.dma("sync", lambda e, outd=outd, o=o, tg=tg: e.dma_start(
                                out=outd[o, :, tg:tg + 512], in_=rot[:]), wait=[ro],
                                tok="rot_free")
                else:
                    for oc in range(2):
                        for tt in range(16):
                            pp = tt % 2
                            for dtl in range(ND):
                                P.on("tensor", lambda e, par=par, pp=pp, dtl=dtl, tt=tt, oc=oc:
                                     e.matmul(
                                         pj[pp][:], lnxT[:, dtl, 128 * tt:128 * (tt + 1)],
                                         wsb[par][:, dtl, 512 * oc:512 * (oc + 1)],
                                         start=(dtl == 0), stop=(dtl == ND - 1)),
                                     wait=[wrdy, P.t(f"pj{pp}_free")] if dtl == 0 else [],
                                     inc="tm" if dtl == ND - 1 else None)
                            mmv = ("tm", P.val("tm"))
                            ev = P.on("scalar", lambda e, pp=pp: e.copy(out=qev[:], in_=pj[pp][:]),
                                      wait=[mmv, P.t("qev_free")], inc="sa", tok=f"pj{pp}_free")
                            tg = t0 + 128 * tt
                            for hh in range(4):
                                h = 8 * oh + 4 * oc + hh
                                last = P.dma("sync", lambda e, h=h, ti=tg // 128, hh=hh:
                                             e.dma_start(
                                                 out=v_d[h, :, ti, :],
                                                 in_=qev[:, 128 * hh:128 * (hh + 1)]),
                                             wait=[ev] if hh == 0 else [])
                            P.tok["qev_free"] = last
                P.tok[f"wsb{par}_free"] = ("tm", P.val("tm"))
            P.tok["lnxT_free"] = ("tm", P.val("tm"))

      P.barrier()

      with contextlib.ExitStack() as bctx:
       if upto not in ("A",):
        bsb = lambda n, sh, d: bctx.enter_context(nc.sbuf_tensor(n, sh, d))
        qT = [bsb(f"qT{i}", [128, T], BF16) for i in range(2)]
        kT = [bsb(f"kT{i}", [128, T], BF16) for i in range(2)]
        vaug = [bsb(f"vaug{i}", [128, NT, 132], BF16) for i in range(2)]
        s2b = [bsb(f"s2b{i}", [128, 512], BF16) for i in range(6)]
        s4b = [bsb(f"s4b{i}", [128, 512], BF16) for i in range(6)]
        dstat = bsb("dstat", [128, 4], F32)
        ctxst = bsb("ctxst", [128, 4, 128], BF16)
        psS = [bctx.enter_context(nc.psum_tensor(f"psS{i}", [128, 512], F32)) for i in range(4)]
        psC = [bctx.enter_context(nc.psum_tensor(f"psC{i}", [128, 512], F32)) for i in range(4)]

        mask_cs = [cs[f"mask{r}"] for r in range(4)]

        for h in range(heads):
            par = h % 2
            qb, kb, vb = qT[par], kT[par], vaug[par]
            P.dma("sync", lambda e, h=h, qb=qb: e.dma_start(out=qb[:], in_=qT_d[h, :, :]),
                  wait=[P.t(f"qkv{par}_free")])
            P.dma("sync", lambda e, h=h, kb=kb: e.dma_start(out=kb[:], in_=kT_d[h, :, :]))
            mz = P.on("vector", lambda e, vb=vb: e.memset(vb[:], 1.0),
                      wait=[P.t(f"qkv{par}_free")], inc="va")
            hl = P.dma("sync", lambda e, h=h, vb=vb: e.dma_start(
                out=vb[:, :, 0:128], in_=v_d[h, :, :, :]),
                wait=[mz], tok="hload")
            hload = ("dS", P.val("dS"))

            for c in range(NCH):
                ntk = 4 * (c + 1)
                LAG = 6
                s4vs = {}

                def emit_ctx(i, ntk=ntk, vb=vb):
                    sp = i % 6
                    for jj in range(4):
                        P.on("tensor", lambda e, sp=sp, jj=jj, i=i, ntk=ntk, vb=vb: e.matmul(
                            psC[jj][:, 0:129], s4b[sp][:, 128 * jj:128 * (jj + 1)],
                            vb[:, i, 0:129], start=(i == 0), stop=(i == ntk - 1)),
                            wait=([s4vs[i]] + ([P.t("ctx_free")] if i == 0 and jj == 0 else []))
                            if jj == 0 else [],
                            inc="tm" if jj == 3 else None,
                            tok=f"s4b{sp}_free" if jj == 3 else None)

                for i in range(ntk):
                    sp = i % 6
                    spS = i % 4
                    P.on("tensor", lambda e, kb=kb, qb=qb, spS=spS, i=i, c=c: e.matmul(
                        psS[spS][:], kb[:, 128 * i:128 * (i + 1)], qb[:, 512 * c:512 * (c + 1)],
                        start=True, stop=True),
                        wait=[hload, P.t(f"psS{spS}_free")], inc="tm")
                    smm = ("tm", P.val("tm"))
                    if i >= LAG:
                        emit_ctx(i - LAG)
                    P.on("scalar", lambda e, sp=sp, spS=spS: e.activation(
                        out=s2b[sp][:], in_=psS[spS][:], func=AF.Square),
                        wait=[smm, P.t(f"s2b{sp}_free")], inc="sa", tok=f"psS{spS}_free")
                    sqv = ("sa", P.val("sa"))
                    wts = [sqv, P.t(f"s4b{sp}_free")]
                    if i >= 4 * c:
                        r = i - 4 * c
                        VC(lambda e, sp=sp, r=r: e.tensor_mul(
                            s2b[sp][:], s2b[sp][:], mask_cs[r][:]), wait=wts)
                        wts = []
                        s4vs[i] = VC(lambda e, sp=sp: e.tensor_mul(
                            s4b[sp][:], s2b[sp][:], s2b[sp][:]),
                            wait=wts, tok=f"s2b{sp}_free")
                    elif i % 2 == 1:
                        s4vs[i] = P.on("gpsimd", lambda e, sp=sp: e.tensor_mul(
                            s4b[sp][:], s2b[sp][:], s2b[sp][:]),
                            wait=wts, inc="pa", tok=f"s2b{sp}_free")
                    else:
                        s4vs[i] = VC(lambda e, sp=sp: e.tensor_mul(
                            s4b[sp][:], s2b[sp][:], s2b[sp][:]),
                            wait=wts, tok=f"s2b{sp}_free")
                for i in range(max(0, ntk - LAG), ntk):
                    emit_ctx(i)
                ctxdone = ("tm", P.val("tm"))
                for jj in range(4):
                    VC(lambda e, jj=jj: e.tensor_scalar_max(
                        dstat[:, jj:jj + 1], psC[jj][:, 128:129], ATT_EPS),
                        wait=[ctxdone, P.t("ctxst_free")] if jj == 0 else [])
                VC(lambda e: e.reciprocal(dstat[:, 0:4], dstat[:, 0:4]))
                dv = None
                for jj in range(4):
                    dv = VC(lambda e, jj=jj: e.tensor_scalar_mul(
                        ctxst[:, jj, :], psC[jj][:, 0:128], dstat[:, jj:jj + 1]))
                P.tok["ctx_free"] = dv
                stv = ("va", P.val("va"))
                P.dma("sync", lambda e, c=c, h=h: e.dma_start(
                    out=ctx_d[h, 512 * c:512 * (c + 1), :].rearrange(
                        "(jj p) hd -> p jj hd", p=128),
                    in_=ctxst[:]), wait=[stv], tok="ctxst_free")
            P.tok[f"qkv{par}_free"] = ("tm", P.val("tm"))

      P.barrier()

      with contextlib.ExitStack() as cctx:
       if upto not in ("A", "B"):
        csb = lambda n, sh, d: cctx.enter_context(nc.sbuf_tensor(n, sh, d))
        ctxc = [csb(f"ctxc{i}", [128, NH, 512], BF16) for i in range(2)]
        woTsC = csb("woTsC", [128, ND, D], BF16)
        xrl = [csb(f"xrl{i}", [128, D], F32) for i in range(2)]
        ao = csb("ao", [128, D], F32)
        x2t = csb("x2t", [128, D], F32)
        ln2t = csb("ln2t", [128, D], BF16)
        stat2 = csb("stat2", [128, 12], F32)
        sqs = csb("sqs", [128, 512], F32)
        psO = [cctx.enter_context(nc.psum_tensor(f"psO{i}", [128, 512], F32)) for i in range(8)]

        wo2 = P.dma("gpsimd", lambda e: e.dma_start(
            out=woTsC[:], in_=woT[:, :].rearrange("(dt p) o -> p dt o", p=128)))

        for c in range(NCH):
            par = c % 2
            cb = ctxc[par]
            for h in range(NH):
                P.dma("scalar", lambda e, c=c, cb=cb, h=h: e.dma_start(
                    out=cb[:, h, :], in_=ctx_d[h, 512 * c:512 * (c + 1), :],
                    transpose=True),
                    wait=[P.t(f"ctxc{par}_free")] if h == 0 else [])
            cc = ("dA", P.val("dA"))
            for tt4 in range(4):
                tt = 4 * c + tt4
                xpar = tt % 2
                xb = xrl[xpar]
                xr = P.dma("sync", lambda e, tt=tt, xb=xb: e.dma_start(
                    out=xb[:], in_=x_in[128 * tt:128 * (tt + 1), :]),
                    wait=[P.t(f"xrl{xpar}_free")])
                ob = 4 * (tt % 2)
                for oc in range(4):
                    for dtl in range(ND):
                        P.on("tensor", lambda e, cb=cb, oc=oc, ob=ob, dtl=dtl, tt4=tt4: e.matmul(
                            psO[ob + oc][:], cb[:, dtl, 128 * tt4:128 * (tt4 + 1)],
                            woTsC[:, dtl, 512 * oc:512 * (oc + 1)],
                            start=(dtl == 0), stop=(dtl == ND - 1)),
                            wait=[cc, wo2, P.t(f"psO{ob + oc}_free")] if dtl == 0 else [],
                            inc="tm" if dtl == ND - 1 else None)
                if tt4 == 3:
                    P.tok[f"ctxc{par}_free"] = ("tm", P.val("tm"))
                omm = ("tm", P.val("tm"))
                sq_done = []
                for oc in range(4):
                    P.on("scalar", lambda e, oc=oc, ob=ob: e.activation(
                        out=sqs[:], in_=psO[ob + oc][:], func=AF.Square,
                        accum_out=stat2[:, 4 + oc:5 + oc]),
                        wait=[omm, P.t("stat2b_free")] if oc == 0 else [],
                        inc="sa")
                    sq_done.append(("sa", P.val("sa")))
                for oc in range(4):
                    VC(lambda e, oc=oc, ob=ob: e.reduce_sum(
                        stat2[:, oc:oc + 1], psO[ob + oc][:], axis=AX.X),
                        wait=[sq_done[oc], P.t("stat2a_free")] if oc == 0 else [sq_done[oc]])
                sqv = ("sa", P.val("sa"))
                VC(lambda e: e.reduce_sum(stat2[:, 8:9], stat2[:, 0:4], axis=AX.X),
                   tok="stat2a_free")
                VC(lambda e: e.reduce_sum(stat2[:, 9:10], stat2[:, 4:8], axis=AX.X),
                   wait=[sqv], tok="stat2b_free")
                VC(lambda e: e.tensor_scalar_mul(stat2[:, 8:9], stat2[:, 8:9], 1.0 / D))
                VC(lambda e: e.tensor_scalar_mul(stat2[:, 9:10], stat2[:, 9:10], 1.0 / D))
                VC(lambda e: e.tensor_mul(stat2[:, 10:11], stat2[:, 8:9], stat2[:, 8:9]))
                VC(lambda e: e.tensor_sub(stat2[:, 9:10], stat2[:, 9:10], stat2[:, 10:11]))
                VC(lambda e: e.tensor_scalar_add(stat2[:, 9:10], stat2[:, 9:10], LN_EPS),
                   tok="varO")
                P.on("scalar", lambda e: e.activation(
                    out=stat2[:, 10:11], in_=stat2[:, 9:10], func=AF.Sqrt),
                    wait=[P.t("varO")], inc="sa", tok="sqrtO")
                VC(lambda e: e.reciprocal(stat2[:, 11:12], stat2[:, 10:11]),
                   wait=[P.t("sqrtO")])
                lastao = None
                for oc in range(4):
                    lastao = VC(lambda e, oc=oc, ob=ob: e.tensor_scalar(
                        ao[:, 512 * oc:512 * (oc + 1)], psO[ob + oc][:], stat2[:, 8:9],
                        stat2[:, 11:12], ALU.subtract, ALU.mult))
                for oc in range(4):
                    P.tok[f"psO{ob + oc}_free"] = lastao
                x2v = VC(lambda e, xb=xb: e.tensor_add(x2t[:], ao[:], xb[:]),
                         wait=[xr, P.t("x2dma"), P.t("x2sq")], tok=f"xrl{xpar}_free")
                P.dma("sync", lambda e, tt=tt: e.dma_start(
                    out=x2_d[128 * tt:128 * (tt + 1), :], in_=x2t[:]), wait=[x2v], tok="x2dma")
                VC(lambda e: e.reduce_sum(stat2[:, 0:1], x2t[:], axis=AX.X))
                for oc in range(4):
                    P.on("scalar", lambda e, oc=oc: e.activation(
                        out=sqs[:], in_=x2t[:, 512 * oc:512 * (oc + 1)], func=AF.Square,
                        accum_out=stat2[:, 4 + oc:5 + oc]),
                        wait=[x2v] if oc == 0 else [], inc="sa" if oc == 3 else None,
                        tok="x2sq" if oc == 3 else None)
                sq2 = ("sa", P.val("sa"))
                VC(lambda e: e.reduce_sum(stat2[:, 5:6], stat2[:, 4:8], axis=AX.X), wait=[sq2])
                VC(lambda e: e.tensor_scalar_mul(stat2[:, 0:1], stat2[:, 0:1], 1.0 / D))
                VC(lambda e: e.tensor_scalar_mul(stat2[:, 5:6], stat2[:, 5:6], 1.0 / D))
                VC(lambda e: e.tensor_mul(stat2[:, 6:7], stat2[:, 0:1], stat2[:, 0:1]))
                VC(lambda e: e.tensor_sub(stat2[:, 5:6], stat2[:, 5:6], stat2[:, 6:7]))
                VC(lambda e: e.tensor_scalar_add(stat2[:, 5:6], stat2[:, 5:6], LN_EPS), tok="var2")
                P.on("scalar", lambda e: e.activation(
                    out=stat2[:, 6:7], in_=stat2[:, 5:6], func=AF.Sqrt),
                    wait=[P.t("var2")], inc="sa", tok="sqrt2")
                VC(lambda e: e.reciprocal(stat2[:, 7:8], stat2[:, 6:7]), wait=[P.t("sqrt2")])
                l2 = VC(lambda e: e.tensor_scalar(
                    ln2t[:], x2t[:], stat2[:, 0:1], stat2[:, 7:8], ALU.subtract, ALU.mult),
                    wait=[P.t("ln2dma")])
                P.dma("sync", lambda e, tt=tt: e.dma_start(
                    out=ln2x_d[128 * tt:128 * (tt + 1), :], in_=ln2t[:]), wait=[l2], tok="ln2dma")

      P.barrier()
    else:
        with contextlib.ExitStack() as ectx:
            esb = lambda n, sh, d: ectx.enter_context(nc.sbuf_tensor(n, sh, d))
            xt0 = esb("xt0", [128, D], F32)
            ln2t0 = esb("ln2t0", [128, D], BF16)
            st0 = esb("st0", [128, 8], F32)
            sqs0 = esb("sqs0", [128, D], F32)
            for tt in range(NT):
                ld = P.dma("sync", lambda e, tt=tt: e.dma_start(
                    out=xt0[:], in_=x_in[128 * tt:128 * (tt + 1), :]),
                    wait=[P.t("xt0_free"), P.t("x2c0")])
                P.dma("sync", lambda e, tt=tt: e.dma_start(
                    out=x2_d[128 * tt:128 * (tt + 1), :], in_=xt0[:]), wait=[ld], tok="x2c0")
                VC(lambda e: e.reduce_sum(st0[:, 0:1], xt0[:], axis=AX.X), wait=[ld])
                P.on("scalar", lambda e: e.activation(
                    out=sqs0[:], in_=xt0[:], func=AF.Square, accum_out=st0[:, 1:2]),
                    wait=[ld], inc="sa")
                sq = ("sa", P.val("sa"))
                VC(lambda e: e.tensor_scalar_mul(st0[:, 2:3], st0[:, 0:1], 1.0 / D))
                VC(lambda e: e.tensor_scalar_mul(st0[:, 3:4], st0[:, 1:2], 1.0 / D), wait=[sq])
                VC(lambda e: e.tensor_mul(st0[:, 4:5], st0[:, 2:3], st0[:, 2:3]))
                VC(lambda e: e.tensor_sub(st0[:, 3:4], st0[:, 3:4], st0[:, 4:5]))
                VC(lambda e: e.tensor_scalar_add(st0[:, 3:4], st0[:, 3:4], LN_EPS), tok="v0")
                P.on("scalar", lambda e: e.activation(out=st0[:, 5:6], in_=st0[:, 3:4], func=AF.Sqrt),
                     wait=[P.t("v0")], inc="sa", tok="s0")
                VC(lambda e: e.reciprocal(st0[:, 6:7], st0[:, 5:6]), wait=[P.t("s0")])
                l2 = VC(lambda e: e.tensor_scalar(
                    ln2t0[:], xt0[:], st0[:, 2:3], st0[:, 6:7], ALU.subtract, ALU.mult),
                    wait=[P.t("ln2dma0")], tok="xt0_free")
                P.dma("sync", lambda e, tt=tt: e.dma_start(
                    out=ln2x_d[128 * tt:128 * (tt + 1), :], in_=ln2t0[:]), wait=[l2], tok="ln2dma0")
        P.barrier()

    if "qt" in dbg:
        P.dma("gpsimd", lambda e: e.dma_start(out=dbg["qt"][:, :], in_=qT_d[0, :, :]))
        P.dma("gpsimd", lambda e: e.dma_start(out=dbg["kt"][:, :], in_=kT_d[0, :, :]))
        P.dma("gpsimd", lambda e: e.dma_start(out=dbg["vt"][:, :], in_=v_d[0, :, :, :]))
    if "ctxt" in dbg:
        P.dma("gpsimd", lambda e: e.dma_start(
            out=dbg["ctxt"][:, :].rearrange("p (jj hd) -> (jj p) hd", hd=HD),
            in_=ctx_d[0, 3584:4096, :]))
    if upto in ("A", "B"):
        if "lnx" in dbg:
            P.dma("gpsimd", lambda e: e.dma_start(out=dbg["lnx"][:, :], in_=lnx_d[:, :]))
        fin = [("dS", P.val("dS")), ("dG", P.val("dG")), ("dA", P.val("dA"))]
        P.on("sync", fn=None, wait=fin)
        P.emit()
        return nc, ctx
    if "x2" in dbg:
        P.dma("gpsimd", lambda e: e.dma_start(out=dbg["x2"][:, :], in_=x2_d[:, :]))
    if "lnx" in dbg:
        P.dma("gpsimd", lambda e: e.dma_start(out=dbg["lnx"][:, :], in_=lnx_d[:, :]))
    if "ln2x" in dbg:
        P.dma("gpsimd", lambda e: e.dma_start(out=dbg["ln2x"][:, :], in_=ln2x_d[:, :]))

    if not with_ffn:
        fin = [("dS", P.val("dS")), ("dG", P.val("dG"))]
        P.on("sync", fn=None, wait=fin)
        P.emit()
        return nc, ctx

    agl = P.on("gpsimd", lambda e: e.collective_compute(
        "AllGather", ALU.bypass, rg, [ln2x_d[:, :].opt()], [ln2x_all_d[:, :].opt()]),
        wait=[("dS", P.val("dS"))], inc="cc")

    with contextlib.ExitStack() as dctx:
        dsb = lambda n, sh, d: dctx.enter_context(nc.sbuf_tensor(n, sh, d))
        scores_all = dsb("scores_all", [128, NT * E], F32)
        mx8_all = dsb("mx8_all", [128, NT * 8], F32)
        wvec_all = dsb("wvec_all", [128, NT * TOPK], F32)
        P_all = dsb("P_all", [128, NT * E], BF16)
        wfin_all = dsb("wfin_all", [128, NT * TOPK], F32)
        gidx_all = dsb("gidx_all", [128, NT * TOPK], I32)
        hist_sb = dsb("hist_sb", [1, NT * E], F32)
        mstat = dsb("mstat", [128, 4], F32)

        with contextlib.ExitStack() as d1c:
            d1sb = lambda n, sh, d: d1c.enter_context(nc.sbuf_tensor(n, sh, d))
            ln2xT = [d1sb(f"ln2xT{i}", [128, ND, 512], BF16) for i in range(2)]
            wgs = d1sb("wgs", [128, ND, E], BF16)
            wsus = d1sb("wsus", [128, ND, 2 * HS], BF16)
            wsds = d1sb("wsds", [128, 4, D], BF16)
            hT = d1sb("hT", [128, 4, 512], BF16)
            gsil = d1sb("gsil", [128, 512], BF16)
            oh6 = d1sb("oh6", [128, E], BF16)
            shst = d1sb("shst", [128, D], BF16)
            zrow = d1sb("zrow", [128, 512], F32)
            psG = d1c.enter_context(nc.psum_tensor("psG", [128, E], F32))
            psH = d1c.enter_context(nc.psum_tensor("psH", [1, E], F32))
            psP = d1c.enter_context(nc.psum_tensor("psP", [128, E], F32))
            psU = [d1c.enter_context(nc.psum_tensor(f"psU{i}", [128, 512], F32)) for i in range(2)]
            psDn = d1c.enter_context(nc.psum_tensor("psDn", [128, 512], F32))

            wg1 = P.dma("gpsimd", lambda e: e.dma_start(
                out=wgs[:], in_=wgate[:, :].rearrange("(dt p) o -> p dt o", p=128)))
            wg2 = P.dma("gpsimd", lambda e: e.dma_start(
                out=wsus[:], in_=wsu[:, :].rearrange("(dt p) o -> p dt o", p=128)))
            wg3 = P.dma("gpsimd", lambda e: e.dma_start(
                out=wsds[:], in_=wsd[:, :].rearrange("(ht p) o -> p ht o", p=128)))
            zz = P.on("vector", lambda e: e.memset(zrow[:], 0.0), inc="va")
            P.dma("sync", lambda e: e.dma_start(
                out=slot_tok_d[0:E * CAP // 128, :].rearrange("(p a) o -> p (a o)", p=128),
                in_=zrow[:]), wait=[zz], tok="slotz")

            for c in range(NCH):
                par = c % 2
                lb = ln2xT[par]
                for ct in range(ND):
                    P.dma("scalar", lambda e, ct=ct, c=c, lb=lb: e.dma_start(
                        out=lb[:, ct, :], in_=ln2x_d[512 * c:512 * (c + 1), 128 * ct:128 * (ct + 1)],
                        transpose=True),
                        wait=[P.t(f"ln2xT{par}_free")] if ct == 0 else [])
                ltv = ("dA", P.val("dA"))
                for ht in range(8):
                    pp = ht % 2
                    for dtl in range(ND):
                        P.on("tensor", lambda e, lb=lb, pp=pp, dtl=dtl, ht=ht: e.matmul(
                            psU[pp][:], wsus[:, dtl, 128 * ht:128 * (ht + 1)], lb[:, dtl, :],
                            start=(dtl == 0), stop=(dtl == ND - 1)),
                            wait=[ltv, wg2, P.t(f"psU{pp}_free")] if dtl == 0 else [],
                            inc="tm" if dtl == ND - 1 else None)
                    suv = ("tm", P.val("tm"))
                    if ht < 4:
                        P.on("scalar", lambda e, pp=pp, ht=ht: e.copy(out=hT[:, ht, :], in_=psU[pp][:]),
                             wait=[suv, P.t(f"hTy{ht}_free")], inc="sa", tok=f"psU{pp}_free")
                    else:
                        P.on("scalar", lambda e, pp=pp: e.activation(
                            out=gsil[:], in_=psU[pp][:], func=AF.Silu),
                            wait=[suv, P.t("gsil_free")], inc="sa", tok=f"psU{pp}_free")
                        sgv = ("sa", P.val("sa"))
                        P.on("vector", lambda e, ht=ht: e.tensor_mul(
                            hT[:, ht - 4, :], hT[:, ht - 4, :], gsil[:]),
                            wait=[sgv], inc="va", tok="gsil_free")
                hTv = ("va", P.val("va"))
                for tt4 in range(4):
                    for oc in range(4):
                        for hs4 in range(4):
                            P.on("tensor", lambda e, hs4=hs4, tt4=tt4, oc=oc: e.matmul(
                                psDn[:], hT[:, hs4, 128 * tt4:128 * (tt4 + 1)],
                                wsds[:, hs4, 512 * oc:512 * (oc + 1)],
                                start=(hs4 == 0), stop=(hs4 == 3)),
                                wait=[hTv, wg3, P.t("psDn_free")] if hs4 == 0 else [],
                                inc="tm" if hs4 == 3 else None)
                        P.on("scalar", lambda e, oc=oc: e.copy(
                            out=shst[:, 512 * oc:512 * (oc + 1)], in_=psDn[:]),
                            wait=[("tm", P.val("tm")), P.t("shst_free")] if oc == 0
                            else [("tm", P.val("tm"))],
                            inc="sa", tok="psDn_free")
                    shv = ("sa", P.val("sa"))
                    P.dma("sync", lambda e, c=c, tt4=tt4: e.dma_start(
                        out=shared_d[128 * (4 * c + tt4):128 * (4 * c + tt4 + 1), :], in_=shst[:]),
                        wait=[shv], tok="shst_free")
                if c == NCH - 1:
                    for ht in range(4):
                        P.tok[f"hTy{ht}_free"] = ("tm", P.val("tm"))
                else:
                    for ht in range(4):
                        P.tok[f"hTy{ht}_free"] = ("tm", P.val("tm"))
                for tt4 in range(4):
                    tt = 4 * c + tt4
                    for dtl in range(ND):
                        P.on("tensor", lambda e, lb=lb, dtl=dtl, tt4=tt4: e.matmul(
                            psG[:], lb[:, dtl, 128 * tt4:128 * (tt4 + 1)], wgs[:, dtl, :],
                            start=(dtl == 0), stop=(dtl == ND - 1)),
                            wait=[ltv, wg1, P.t("psG_free")] if dtl == 0 else [],
                            inc="tm" if dtl == ND - 1 else None)
                    gmm = ("tm", P.val("tm"))
                    P.on("scalar", lambda e, tt=tt: e.activation(
                        out=scores_all[:, E * tt:E * (tt + 1)], in_=psG[:], func=AF.Sigmoid),
                        wait=[gmm], inc="sa", tok="psG_free")
                    sgv = ("sa", P.val("sa"))
                    VC(lambda e, tt=tt: e.max(
                        out=mx8_all[:, 8 * tt:8 * (tt + 1)], in_=scores_all[:, E * tt:E * (tt + 1)]),
                        wait=[sgv])
                    VC(lambda e, tt=tt: e.reduce_sum(
                        mstat[:, 0:1], mx8_all[:, 8 * tt:8 * tt + 6], axis=AX.X))
                    VC(lambda e: e.reciprocal(mstat[:, 1:2], mstat[:, 0:1]))
                    VC(lambda e, tt=tt: e.tensor_scalar_mul(
                        wvec_all[:, TOPK * tt:TOPK * (tt + 1)], mx8_all[:, 8 * tt:8 * tt + 6],
                        mstat[:, 1:2]))
                    ohv = VC(lambda e, tt=tt: e.tensor_single_scalar(
                        oh6[:], scores_all[:, E * tt:E * (tt + 1)],
                        mx8_all[:, 8 * tt + 5:8 * tt + 6], ALU.is_ge),
                        wait=[P.t("oh6_free")])
                    P.on("tensor", lambda e: e.matmul(
                        psH[:], cs["ones_col"][:], oh6[:], start=True, stop=True),
                        wait=[ohv, tok_const, P.t("psH_free")], inc="tm")
                    hmm = ("tm", P.val("tm"))
                    P.on("tensor", lambda e: e.matmul(
                        psP[:], cs["lexclT"][:], oh6[:], start=True, stop=True),
                        wait=[P.t("psP_free")], inc="tm", tok="oh6_free")
                    pmm = ("tm", P.val("tm"))
                    P.on("vector", lambda e, tt=tt: e.tensor_copy(
                        hist_sb[0:1, E * tt:E * (tt + 1)], psH[:]),
                        wait=[hmm], inc="va", tok="psH_free")
                    P.on("vector", lambda e, tt=tt: e.tensor_copy(
                        P_all[:, E * tt:E * (tt + 1)], psP[:]),
                        wait=[pmm], inc="va", tok="psP_free")
            hsv = ("va", P.val("va"))
            P.dma("sync", lambda e: e.dma_start(
                out=hist_d[:, :].rearrange("a b -> (a b)"), in_=hist_sb[0:1, :]),
                wait=[hsv], tok="hist_out")

        with contextlib.ExitStack() as d2c:
            d2sb = lambda n, sh, d: d2c.enter_context(nc.sbuf_tensor(n, sh, d))
            hall = [d2sb(f"hall{i}", [128, E], F32) for i in range(2)]
            lexf = d2sb("lexf", [128, 128], F32)
            onescol_f = d2sb("onescol_f", [128, 1], F32)
            tot0 = d2sb("tot0", [1, E], F32)
            basev = d2sb("basev", [128, E], F32)
            PF = d2sb("PF", [128, E], F32)
            PFE_s = d2sb("PFE_s", [128, E], F32)
            PFE_g = d2sb("PFE_g", [128, E], F32)
            pen = d2sb("pen", [128, E], F32)
            ohk = d2sb("ohk", [128, E], F32)
            prodb = d2sb("prodb", [128, E], F32)
            slotf = d2sb("slotf", [128, 2], F32)
            dumpv = d2sb("dumpv", [128, 1], F32)
            vmask = d2sb("vmask", [128, 1], I32)
            sloti = d2sb("sloti", [128, TOPK], I32)
            tids = d2sb("tids", [128, 1], F32)
            sel32 = d2sb("sel32_sb", [NT, NT * 128], F32)
            psB = d2c.enter_context(nc.psum_tensor("psB", [128, E], F32))
            psBC = d2c.enter_context(nc.psum_tensor("psBC", [128, E], F32))
            s32v = P.dma("sync", lambda e: e.dma_start(out=sel32[:], in_=cin["sel32"][:, :]))

            agh = P.on("gpsimd", lambda e: e.collective_compute(
                "AllGather", ALU.bypass, rg, [hist_d[:, :].opt()], [hist_all_d[:, :].opt()]),
                wait=[P.t("hist_out")], inc="cc")
            lexv = P.on("vector", lambda e: e.tensor_copy(lexf[:], cs["lexclT"][:]),
                        wait=[tok_const, ("sa", P.val("sa"))], inc="va")
            onev = P.on("vector", lambda e: e.memset(onescol_f[:], 1.0), inc="va")
            for i in range(2):
                P.dma("sync", lambda e, i=i: e.dma_start(
                    out=hall[i][:], in_=hist_all_d[128 * i:128 * (i + 1), :]), wait=[agh])
            hav = ("dS", P.val("dS"))
            P.on("tensor", lambda e: e.matmul(psB[:], lexf[:], hall[0][:], start=True, stop=True),
                 wait=[hav, lexv], inc="tm")
            b0 = P.on("vector", lambda e: e.tensor_copy(basev[:], psB[:]),
                      wait=[("tm", P.val("tm"))], inc="va")
            d0 = P.dma("sync", lambda e: e.dma_start(out=base_d[0:128, :], in_=basev[:]), wait=[b0])
            P.on("tensor", lambda e: e.matmul(psBC[0:1, :], onescol_f[:], hall[0][:],
                                              start=True, stop=True), wait=[onev, hav], inc="tm")
            t0v = P.on("vector", lambda e: e.tensor_copy(tot0[:], psBC[0:1, :]),
                       wait=[("tm", P.val("tm"))], inc="va")
            P.on("tensor", lambda e: e.matmul(psB[:], lexf[:], hall[1][:], start=True, stop=False),
                 wait=[b0], inc="tm")
            P.on("tensor", lambda e: e.matmul(psB[:], cs["ones_row_f"][:, :], tot0[:],
                                              start=False, stop=True),
                 wait=[t0v], inc="tm")
            b1 = P.on("vector", lambda e: e.tensor_copy(basev[:], psB[:]),
                      wait=[("tm", P.val("tm")), d0], inc="va")
            P.dma("sync", lambda e: e.dma_start(out=base_d[128:256, :], in_=basev[:]), wait=[b1])
            bdv = ("dS", P.val("dS"))
            bgx = d2sb("bgx", [NT, 1], I32)
            base_own = d2sb("base_own", [NT, E], F32)
            P.dma("sync", lambda e: e.dma_start(out=bgx[:], in_=cin["base_gidx"][:, :]))
            bgv = ("dS", P.val("dS"))
            P.dma("gpsimd", lambda e: e.indirect_dma_start(
                out=base_own[:, :], out_offset=None,
                in_=base_d[:, :], in_offset=IOA(bgx[:, :], 0)),
                wait=[bdv, bgv])
            brv = ("dG", P.val("dG"))

            for tt in range(NT):
                P.on("tensor", lambda e, tt=tt: e.matmul(
                    psBC[:], sel32[:, 128 * tt:128 * (tt + 1)], base_own[:, :],
                    start=True, stop=True),
                    wait=[brv, s32v, P.t("psBC_free")], inc="tm")
                bcv = ("tm", P.val("tm"))
                VC(lambda e, tt=tt: e.tensor_tensor(
                    PF[:], psBC[:], P_all[:, E * tt:E * (tt + 1)], ALU.add),
                    wait=[bcv], tok="psBC_free")
                VC(lambda e: e.tensor_single_scalar(pen[:], PF[:], float(CAP), ALU.is_ge))
                VC(lambda e: e.tensor_add(PFE_s[:], PF[:], cs["e1024"][:]))
                VC(lambda e: e.scalar_tensor_tensor(
                    PFE_s[:], pen[:], BIG, PFE_s[:], ALU.mult, ALU.add))
                VC(lambda e: e.tensor_add(PFE_g[:], PF[:], cs["gcomb"][:]))
                VC(lambda e: e.scalar_tensor_tensor(
                    PFE_g[:], pen[:], BIG, PFE_g[:], ALU.mult, ALU.add))
                VC(lambda e, tt=tt: e.tensor_scalar(
                    tids[:], cs["iota_p"][:], cs["core_base"][:, 0:1], float(128 * tt),
                    ALU.add, ALU.add), wait=[P.t("tids_free")])
                for k in range(TOPK):
                    VC(lambda e, tt=tt, k=k: e.tensor_single_scalar(
                        ohk[:], scores_all[:, E * tt:E * (tt + 1)],
                        mx8_all[:, 8 * tt + k:8 * tt + k + 1], ALU.is_equal))
                    VC(lambda e: e.tensor_mul(prodb[:], ohk[:], PFE_s[:]))
                    VC(lambda e: e.reduce_sum(slotf[:, 0:1], prodb[:], axis=AX.X))
                    VC(lambda e: e.tensor_single_scalar(
                        slotf[:, 1:2], slotf[:, 0:1], 65536.0, ALU.is_lt))
                    VC(lambda e, tt=tt, k=k: e.tensor_mul(
                        wfin_all[:, TOPK * tt + k:TOPK * tt + k + 1],
                        wvec_all[:, TOPK * tt + k:TOPK * tt + k + 1], slotf[:, 1:2]))
                    VC(lambda e, tt=tt, k=k: e.tensor_scalar_add(
                        dumpv[:], cs["iota_p"][:], float(E * CAP + 128 * (TOPK * tt + k))))
                    VC(lambda e: e.tensor_single_scalar(
                        vmask[:], slotf[:, 0:1], 65536.0, ALU.is_lt))
                    VC(lambda e: e.select(
                        out=dumpv[:], mask=vmask[:], on_true=slotf[:, 0:1],
                        on_false=dumpv[:]))
                    sv = VC(lambda e, k=k: e.tensor_copy(sloti[:, k:k + 1], dumpv[:]),
                            wait=[P.t("sloti_free")] if k == 0 else [])
                    VC(lambda e: e.tensor_mul(prodb[:], ohk[:], PFE_g[:]))
                    VC(lambda e: e.reduce_sum(slotf[:, 1:2], prodb[:], axis=AX.X))
                    VC(lambda e, tt=tt, k=k: e.tensor_scalar_min(
                        gidx_all[:, TOPK * tt + k:TOPK * tt + k + 1], slotf[:, 1:2],
                        float(E * CAP - 1)))
                    P.dma("gpsimd", lambda e, k=k: e.indirect_dma_start(
                        out=slot_tok_d[:, :], out_offset=IOA(sloti[:, k:k + 1], 1),
                        in_=tids[:, :], in_offset=None),
                        wait=[sv, P.t("slotz")] if k == 0 else [sv])
                P.tok["sloti_free"] = ("dG", P.val("dG"))
                P.tok["tids_free"] = ("dG", P.val("dG"))

            nsl = E * CAP // 128
            slsum = d2sb("slsum", [128, nsl // 128, 128], F32)
            slpart = d2sb("slpart", [128, nsl // 128, 128], F32)
            sgx = d2sb("sgx", [64, 1], I32)
            slmine = d2sb("slmine", [64, 128], F32)
            cpv = P.dma("sync", lambda e: e.dma_start(
                out=slot_rs_d[:, :], in_=slot_tok_d[0:nsl, :]),
                wait=[("dG", P.val("dG"))])
            agsl = P.on("gpsimd", lambda e: e.collective_compute(
                "AllGather", ALU.bypass, rg, [slot_rs_d[:, :].opt()],
                [slot_all8_d[:, :].opt()]),
                wait=[cpv], inc="cc")
            l0 = P.dma("sync", lambda e: e.dma_start(
                out=slsum[:], in_=slot_all8_d[0:nsl, :].rearrange("(a p) c -> p a c", p=128)),
                wait=[agsl])
            addtoks = [l0]
            for rr in range(1, NCORES):
                lr = P.dma("sync", lambda e, rr=rr: e.dma_start(
                    out=slpart[:],
                    in_=slot_all8_d[nsl * rr:nsl * (rr + 1), :].rearrange(
                        "(a p) c -> p a c", p=128)),
                    wait=[P.t("slpart_used")] if rr >= 2 else [])
                VC(lambda e: e.tensor_add(slsum[:], slsum[:], slpart[:]),
                   wait=[lr] + ([l0] if rr == 1 else []), tok="slpart_used")
            ssv = P.dma("sync", lambda e: e.dma_start(
                out=slot_sum_d[:, :].rearrange("(a p) c -> p a c", p=128), in_=slsum[:]),
                wait=[P.t("slpart_used")])
            sgv = P.dma("sync", lambda e: e.dma_start(out=sgx[:], in_=cin["slot_gidx"][:, :]))
            gmv = P.dma("gpsimd", lambda e: e.indirect_dma_start(
                out=slmine[:, :], out_offset=None,
                in_=slot_sum_d[:, :], in_offset=IOA(sgx[:, :], 0)),
                wait=[ssv, sgv])
            rsv = P.dma("sync", lambda e: e.dma_start(out=slot_own_d[:, :], in_=slmine[:]),
                        wait=[gmv])

        if "scores" in dbg:
            P.dma("gpsimd", lambda e: e.dma_start(out=dbg["scores"][:, :], in_=scores_all[:]))
        if "slot_tok" in dbg:
            P.dma("gpsimd", lambda e: e.dma_start(out=dbg["slot_tok"][:, :].rearrange("(a b) o -> a (b o)", b=128), in_=slot_tok_d[0:E * CAP // 128, :]),
                  wait=[rsv])
        if "wfin" in dbg:
            P.dma("gpsimd", lambda e: e.dma_start(out=dbg["wfin"][:, :], in_=wfin_all[:]))
        if "gidx" in dbg:
            P.dma("gpsimd", lambda e: e.dma_start(out=dbg["gidx"][:, :], in_=gidx_all[:]))
        if "shared" in dbg:
            P.dma("gpsimd", lambda e: e.dma_start(out=dbg["shared"][:, :], in_=shared_d[:, :]))
        if "hist_all" in dbg:
            P.dma("gpsimd", lambda e: e.dma_start(out=dbg["hist_all"][:, :], in_=hist_all_d[:, :]))
        if "base" in dbg:
            P.dma("gpsimd", lambda e: e.dma_start(out=dbg["base"][:, :], in_=base_d[:, :]))

        with contextlib.ExitStack() as d3c:
            d3sb = lambda n, sh, d: d3c.enter_context(nc.sbuf_tensor(n, sh, d))
            slf = d3sb("slf", [128, 8], F32)
            sli = d3sb("sli", [128, 8], I32)
            xg2 = [d3sb(f"xg2{i}", [128, 8, D], BF16) for i in range(2)]
            xdT = [d3sb("xdT0", [128, ND, CAP], BF16)]
            wupb = [d3sb(f"wupb{i}", [128, ND, 2 * H_EXP], BF16) for i in range(2)]
            wdnb = [d3sb(f"wdnb{i}", [128, 2, D], BF16) for i in range(2)]
            hrT = d3sb("hrT", [128, 2, CAP], BF16)
            gpart = d3sb("gpart", [128, 2, CAP], BF16)
            odst = d3sb("odst", [128, D], BF16)
            psE = [d3c.enter_context(nc.psum_tensor(f"psE{i}", [128, 512], F32)) for i in range(2)]

            gavs = []
            for el in range(ELOC):
                par = el % 2
                xg = xg2[par]
                for j in range(8):
                    sl = P.dma("sync", lambda e, el=el, j=j: e.dma_start(
                        out=slf[:, j:j + 1],
                        in_=slot_own_d[8 * el + j:8 * el + j + 1, :]),
                        wait=[rsv, P.t("slf_free")] if j == 0 else [])
                sliv = P.on("vector", lambda e: e.tensor_copy(sli[:], slf[:]),
                            wait=[sl, P.t("sli_free")], inc="va", tok="slf_free")
                for j in range(8):
                    gj = P.dma("gpsimd", lambda e, j=j, xg=xg: e.indirect_dma_start(
                        out=xg[:, j, :], out_offset=None,
                        in_=ln2x_all_d[:, :], in_offset=IOA(sli[:, j:j + 1], 0)),
                        wait=[sliv, agl, P.t(f"xg{par}_free")] if j == 0 else [])
                    P.dma("sync", lambda e, el=el, j=j, xg=xg: e.dma_start(
                        out=xdisp_d[CAP * el + 128 * j:CAP * el + 128 * (j + 1), :],
                        in_=xg[:, j, :]), wait=[gj])
                gavs.append(("dS", P.val("dS")))
                P.tok["sli_free"] = ("dG", P.val("dG"))
                P.tok[f"xg{par}_free"] = ("dS", P.val("dS"))

            for el in range(ELOC):
                par = el % 2
                wub, wdb, xb = wupb[par], wdnb[par], xdT[0]
                wu = P.dma("gpsimd", lambda e, el=el, wub=wub: e.dma_start(
                    out=wub[:], in_=wup[el, :, :].rearrange("(dt p) o -> p dt o", p=128)),
                    wait=[P.t(f"wupb{par}_free")])
                wd = P.dma("gpsimd", lambda e, el=el, wdb=wdb: e.dma_start(
                    out=wdb[:], in_=wdn[el, :, :].rearrange("(ht p) o -> p ht o", p=128)),
                    wait=[P.t(f"wdnb{par}_free")])
                for ct in range(ND):
                    P.dma("scalar", lambda e, el=el, ct=ct, xb=xb: e.dma_start(
                        out=xb[:, ct, :],
                        in_=xdisp_d[CAP * el:CAP * (el + 1), 128 * ct:128 * (ct + 1)],
                        transpose=True),
                        wait=[gavs[el], P.t("xdT_free")] if ct == 0 else [])
                xtv = ("dA", P.val("dA"))
                for ht in range(4):
                    for sc in range(2):
                        pp = sc
                        for dtl in range(ND):
                            P.on("tensor", lambda e, wub=wub, xb=xb, pp=pp, dtl=dtl, ht=ht, sc=sc:
                                 e.matmul(
                                     psE[pp][:], wub[:, dtl, 128 * ht:128 * (ht + 1)],
                                     xb[:, dtl, 512 * sc:512 * (sc + 1)],
                                     start=(dtl == 0), stop=(dtl == ND - 1)),
                                 wait=[xtv, wu, P.t(f"psE{pp}_free")] if dtl == 0 else [],
                                 inc="tm" if dtl == ND - 1 else None)
                        umm = ("tm", P.val("tm"))
                        if ht < 2:
                            P.on("scalar", lambda e, pp=pp, ht=ht, sc=sc: e.activation(
                                out=gpart[:, ht, 512 * sc:512 * (sc + 1)], in_=psE[pp][:],
                                func=AF.Silu),
                                wait=[umm, P.t("gpart_free")] if (ht, sc) == (0, 0) else [umm],
                                inc="sa", tok=f"psE{pp}_free")
                        else:
                            P.on("scalar", lambda e, pp=pp, ht=ht, sc=sc: e.copy(
                                out=hrT[:, ht - 2, 512 * sc:512 * (sc + 1)], in_=psE[pp][:]),
                                wait=[umm, P.t("hrT_free")] if (ht, sc) == (2, 0) else [umm],
                                inc="sa", tok=f"psE{pp}_free")
                P.tok[f"wupb{par}_free"] = ("tm", P.val("tm"))
                P.tok["xdT_free"] = ("tm", P.val("tm"))
                hupv = ("sa", P.val("sa"))
                P.on("vector", lambda e: e.tensor_mul(hrT[:, 0, :], hrT[:, 0, :], gpart[:, 0, :]),
                     wait=[hupv])
                hv = P.on("vector", lambda e: e.tensor_mul(
                    hrT[:, 1, :], hrT[:, 1, :], gpart[:, 1, :]), inc="va", tok="gpart_free")
                for j in range(8):
                    for oc in range(4):
                        pp = (j * 4 + oc) % 2
                        for h2 in range(2):
                            P.on("tensor", lambda e, wdb=wdb, pp=pp, h2=h2, j=j, oc=oc: e.matmul(
                                psE[pp][:], hrT[:, h2, 128 * j:128 * (j + 1)],
                                wdb[:, h2, 512 * oc:512 * (oc + 1)],
                                start=(h2 == 0), stop=(h2 == 1)),
                                wait=[hv, wd, P.t(f"psE{pp}_free")] if h2 == 0 else [],
                                inc="tm" if h2 == 1 else None)
                        dmm = ("tm", P.val("tm"))
                        P.on("scalar", lambda e, pp=pp, oc=oc: e.copy(
                            out=odst[:, 512 * oc:512 * (oc + 1)], in_=psE[pp][:]),
                            wait=[dmm, P.t("odst_free")] if oc == 0 else [dmm],
                            inc="sa", tok=f"psE{pp}_free")
                    ov = ("sa", P.val("sa"))
                    P.dma("sync", lambda e, el=el, j=j: e.dma_start(
                        out=odisp_d[CAP * el + 128 * j:CAP * el + 128 * (j + 1), :], in_=odst[:]),
                        wait=[ov], tok="odst_free")
                P.tok[f"wdnb{par}_free"] = ("tm", P.val("tm"))
                P.tok["hrT_free"] = ("tm", P.val("tm"))
                if el == 3:
                    P.on("gpsimd", lambda e: e.collective_compute(
                        "AllGather", ALU.bypass, rg, [odisp_d[0:4 * CAP, :].opt()],
                        [odall_d[0:NCORES * 4 * CAP, :].opt()]),
                        wait=[("dS", P.val("dS"))], inc="cc")

        if "odisp" in dbg:
            P.dma("gpsimd", lambda e: e.dma_start(out=dbg["odisp"][:, :], in_=odisp_d[:, :]),
                  wait=[("dS", P.val("dS"))])

        ago = P.on("gpsimd", lambda e: e.collective_compute(
            "AllGather", ALU.bypass, rg, [odisp_d[4 * CAP:8 * CAP, :].opt()],
            [odall_d[NCORES * 4 * CAP:, :].opt()]),
            wait=[("dS", P.val("dS"))], inc="cc")

        with contextlib.ExitStack() as d4c:
            d4sb = lambda n, sh, d: d4c.enter_context(nc.sbuf_tensor(n, sh, d))
            gt = [d4sb(f"gt{i}", [128, D], BF16) for i in range(6)]
            acc = d4sb("acc", [128, D], BF16)
            acc2 = d4sb("acc2", [128, D], BF16)
            sht = d4sb("sht", [128, D], BF16)
            x2r = [d4sb(f"x2r{i}", [128, D], F32) for i in range(2)]
            ffo = d4sb("ffo", [128, D], F32)
            yst = d4sb("yst", [128, D], F32)
            st4 = d4sb("st4", [128, 12], F32)
            sqs4 = d4sb("sqs4", [128, 512], F32)

            for i in range(6):
                P.on("vector", lambda e, i=i: e.memset(gt[i][:], 0.0), inc="va",
                     tok=f"gt{i}_init")

            for tt in range(NT):
                xpar = tt % 2
                xb = x2r[xpar]
                xr = P.dma("sync", lambda e, tt=tt, xb=xb: e.dma_start(
                    out=xb[:], in_=x2_d[128 * tt:128 * (tt + 1), :]),
                    wait=[P.t(f"x2r{xpar}_free")])
                sh = P.dma("sync", lambda e, tt=tt: e.dma_start(
                    out=sht[:], in_=shared_d[128 * tt:128 * (tt + 1), :]),
                    wait=[P.t("sht_free")] + ([("dS", P.val("dS"))] if tt == 0 else []))
                gvs = {}

                def emit_gather(k, tt=tt):
                    gp = k % 6
                    gb = gt[gp]
                    gvs[k] = P.dma("gpsimd", lambda e, gb=gb, tt=tt, k=k: e.indirect_dma_start(
                        out=gb[:, :], out_offset=None,
                        in_=odall_d[:, :],
                        in_offset=IOA(gidx_all[:, TOPK * tt + k:TOPK * tt + k + 1], 0)),
                        wait=[ago, P.t(f"gt{gp}_free"), P.t(f"gt{gp}_init")])

                for k in range(TOPK):
                    emit_gather(k)
                for k in range(TOPK):
                    gp = k % 6
                    gb = gt[gp]
                    if k == 0:
                        av = VC(lambda e, tt=tt, k=k, gb=gb: e.tensor_scalar_mul(
                            acc[:], gb[:], wfin_all[:, TOPK * tt + k:TOPK * tt + k + 1]),
                            wait=[gvs[k], P.t("acc_free")])
                    else:
                        av = VC(lambda e, tt=tt, k=k, gb=gb: e.scalar_tensor_tensor(
                            acc[:], gb[:], wfin_all[:, TOPK * tt + k:TOPK * tt + k + 1],
                            acc[:], ALU.mult, ALU.add), wait=[gvs[k]])
                    P.tok[f"gt{gp}_free"] = av
                fv = VC(lambda e: e.tensor_add(ffo[:], acc[:], sht[:]),
                        wait=[sh, P.t("ffo_sq")], tok="acc_free")
                P.tok["sht_free"] = fv
                VC(lambda e: e.reduce_sum(st4[:, 0:1], ffo[:], axis=AX.X), wait=[fv])
                for oc in range(4):
                    P.on("scalar", lambda e, oc=oc: e.activation(
                        out=sqs4[:], in_=ffo[:, 512 * oc:512 * (oc + 1)], func=AF.Square,
                        accum_out=st4[:, 4 + oc:5 + oc]),
                        wait=[fv] if oc == 0 else [], inc="sa" if oc == 3 else None,
                        tok="ffo_sq" if oc == 3 else None)
                sq4 = ("sa", P.val("sa"))
                VC(lambda e: e.reduce_sum(st4[:, 1:2], st4[:, 4:8], axis=AX.X), wait=[sq4])
                VC(lambda e: e.tensor_scalar_mul(st4[:, 0:1], st4[:, 0:1], 1.0 / D))
                VC(lambda e: e.tensor_scalar_mul(st4[:, 1:2], st4[:, 1:2], 1.0 / D))
                VC(lambda e: e.tensor_mul(st4[:, 2:3], st4[:, 0:1], st4[:, 0:1]))
                VC(lambda e: e.tensor_sub(st4[:, 1:2], st4[:, 1:2], st4[:, 2:3]))
                VC(lambda e: e.tensor_scalar_add(st4[:, 1:2], st4[:, 1:2], LN_EPS), tok="var4")
                P.on("scalar", lambda e: e.activation(out=st4[:, 2:3], in_=st4[:, 1:2], func=AF.Sqrt),
                     wait=[P.t("var4")], inc="sa", tok="sqrt4")
                VC(lambda e: e.reciprocal(st4[:, 3:4], st4[:, 2:3]), wait=[P.t("sqrt4")])
                VC(lambda e: e.tensor_scalar(
                    yst[:], ffo[:], st4[:, 0:1], st4[:, 3:4], ALU.subtract, ALU.mult),
                    wait=[P.t("yst_free")])
                yv = VC(lambda e, xb=xb: e.tensor_add(yst[:], yst[:], xb[:]),
                        wait=[xr], tok=f"x2r{xpar}_free")
                P.dma("sync", lambda e, tt=tt: e.dma_start(
                    out=y_out[128 * tt:128 * (tt + 1), :], in_=yst[:]), wait=[yv], tok="yst_free")

    fin = [("dS", P.val("dS")), ("dG", P.val("dG")), ("dA", P.val("dA"))]
    P.on("sync", fn=None, wait=fin)
    P.emit()
    return nc, ctx


def kernel(L0_attn_x, L0_attn_Wq, L0_attn_Wk, L0_attn_Wv, L0_attn_Wo,
           L0_ffn_w_gate, L0_ffn_w_shared_up, L0_ffn_w_shared_down,
           L0_ffn_w_up, L0_ffn_w_down, _cfg=None):
    cfg = dict(_cfg or {})
    nc, ctx = build_program(cfg)

    wqT = np.ascontiguousarray(np.asarray(L0_attn_Wq).T)
    wkT = np.ascontiguousarray(np.asarray(L0_attn_Wk).T)
    wvT = np.ascontiguousarray(np.asarray(L0_attn_Wv).T)
    woT = np.ascontiguousarray(np.asarray(L0_attn_Wo).T)
    x = np.asarray(L0_attn_x)
    wup = np.asarray(L0_ffn_w_up)
    wdn = np.asarray(L0_ffn_w_down)

    in_maps = []
    for c in range(NCORES):
        m = {
            "x": np.ascontiguousarray(x[c]),
            "wqT": wqT, "wkT": wkT, "wvT": wvT, "woT": woT,
            "wgate": np.ascontiguousarray(np.asarray(L0_ffn_w_gate)),
            "wsu": np.ascontiguousarray(np.asarray(L0_ffn_w_shared_up)),
            "wsd": np.ascontiguousarray(np.asarray(L0_ffn_w_shared_down)),
            "wup": np.ascontiguousarray(wup[ELOC * c:ELOC * (c + 1)]),
            "wdn": np.ascontiguousarray(wdn[ELOC * c:ELOC * (c + 1)]),
        }
        m.update(make_host_constants(c))
        in_maps.append(m)

    res = run_bass_kernel_spmd(nc, in_maps, core_ids=list(range(NCORES)))
    ctx.close()
    out = np.stack([res.results[c]["y"] for c in range(NCORES)], axis=0)
    if cfg.get("return_raw"):
        return out, res
    return out
